# version 02
# baseline (speedup 1.0000x reference)
import numpy as np
from contextlib import ExitStack
import concourse.bass as bass
import concourse.mybir as mybir
from concourse.bass_utils import run_bass_kernel_spmd

F32 = mybir.dt.float32
BF16 = mybir.dt.bfloat16
AF = mybir.ActivationFunctionType
ALU = mybir.AluOpType
AX = mybir.AxisListType

NCORES = 8
B, L, D = 2, 8192, 1024
TPC = 2048
NCH = D // 128

SKIP_WAITS = False
LEAN_DRAIN = True
COMPUTE_QUEUES = ("tensor", "vector", "scalar", "pool")
DMA_QUEUES = ("qs", "qg", "qa")
ALL_QUEUES = COMPUTE_QUEUES + DMA_QUEUES


def _tags(*aps):
    out = []
    for a in aps:
        if a is None or isinstance(a, (int, float)):
            continue
        if isinstance(a, (list, tuple)):
            out.extend(_tags(*a))
        elif isinstance(a, str):
            out.append(a)
        elif hasattr(a, "tensor"):
            out.append(a.tensor.name)
    return out


class Prog:

    def __init__(self, nc):
        self.nc = nc
        self.ops = {q: [] for q in ALL_QUEUES}
        self.count = {q: 0 for q in ALL_QUEUES}
        self.last_write = {}
        self.readers = {}
        self.gidx = 0
        self.pfx = ""
        self._scopes = []
        self.regs = {}

    def open_scope(self):
        self._scopes.append(ExitStack())

    def close_scope(self):
        self._scopes.pop().close()

    def sb(self, name, shape, dtype):
        if self._scopes:
            return self._scopes[-1].enter_context(
                self.nc.sbuf_tensor(self.pfx + name, shape, dtype))
        return self.nc.alloc_sbuf_tensor(self.pfx + name, shape, dtype)

    def ps(self, name, shape, dtype=F32):
        if self._scopes:
            return self._scopes[-1].enter_context(
                self.nc.psum_tensor(self.pfx + name, shape, dtype))
        return self.nc.alloc_psum_tensor(self.pfx + name, shape, dtype)

    def _edge_needed(self, q, src_q):
        if src_q == q and q == "tensor":
            return False
        return True

    def op(self, q, fn, reads=(), writes=(), inc_n=None, drain_ok=True):
        if inc_n is None:
            inc_n = 16 if q in DMA_QUEUES else 1
        waits = {}
        for t in reads:
            lw = self.last_write.get(t)
            if lw and self._edge_needed(q, lw[0]):
                waits[lw[0]] = max(waits.get(lw[0], 0), lw[1])
        for t in writes:
            lw = self.last_write.get(t)
            if lw and self._edge_needed(q, lw[0]):
                waits[lw[0]] = max(waits.get(lw[0], 0), lw[1])
            for (rq, rv) in self.readers.get(t, ()):
                if self._edge_needed(q, rq):
                    waits[rq] = max(waits.get(rq, 0), rv)
        self.count[q] += inc_n
        val_after = self.count[q]
        for t in reads:
            self.readers.setdefault(t, []).append((q, val_after))
        for t in writes:
            self.last_write[t] = (q, val_after)
            self.readers[t] = []
        self.ops[q].append((sorted(waits.items()), fn, q, self.gidx, inc_n,
                            drain_ok))
        self.gidx += 1

    def emit(self):
        nc = self.nc
        active = [q for q in ALL_QUEUES if self.ops[q]]
        with ExitStack() as es:
            sems = {q: es.enter_context(nc.semaphore(f"sem_{q}"))
                    for q in active}
            block = es.enter_context(nc.Block())

            waited_vals = {q: set() for q in active}
            for q in active:
                for waits, _f, qn, _g, _i, _d in self.ops[q]:
                    for wq, wv in waits:
                        if wq != qn:
                            waited_vals[wq].add(wv)
                waited_vals[q].add(self.count[q])

            def runner(qnames, finish=False):
                merged = []
                for qn in qnames:
                    merged.extend(self.ops[qn])
                merged.sort(key=lambda t: t[3])

                def body(eng):
                    cur = {}
                    cnt = {qn: 0 for qn in qnames}
                    for waits, fn, qn, _gi, inc_n, drain_ok in merged:
                        for wq, wv in waits:
                            if SKIP_WAITS:
                                if wq == qn:
                                    continue
                                if cur.get(wq, -1) >= wv:
                                    continue
                            eng.wait_ge(sems[wq], wv)
                            cur[wq] = wv
                        cnt[qn] += inc_n
                        try:
                            inst = fn(eng)
                        except Exception:
                            print(f"EMIT FAIL q={qn} gidx={_gi}")
                            raise
                        if inc_n == 0:
                            continue
                        if (drain_ok and qn in ("scalar", "vector", "pool")
                                and (not LEAN_DRAIN or cnt[qn] in waited_vals[qn])):
                            eng.drain().then_inc(sems[qn], inc_n)
                        else:
                            inst.then_inc(sems[qn], inc_n)
                    if finish:
                        for q2 in active:
                            eng.wait_ge(sems[q2], self.count[q2])
                return body

            groups = {block.tensor: ["tensor"], block.vector: ["vector"],
                      block.scalar: ["scalar", "qa"],
                      block.gpsimd: ["pool", "qg"]}
            for sect, qs_ in groups.items():
                qs_ = [q for q in qs_ if q in active]
                if qs_:
                    sect(runner(qs_))
            if "qs" in active:
                block.sync(runner(["qs"], finish=True))
            else:
                raise RuntimeError("programs must use the qs DMA queue")

    def dma(self, q, out, in_, reads=(), writes=(), **kw):
        self.op(q, lambda e: e.dma_start(out=out, in_=in_, **kw),
                reads=_tags(in_) + list(reads), writes=_tags(out) + list(writes))

    def mm(self, out, lhsT, rhs, start=True, stop=True, reads=(), writes=()):
        self.op("tensor",
                lambda e: e.matmul(out, lhsT, rhs, start=start, stop=stop),
                reads=_tags(lhsT, rhs) + list(reads),
                writes=_tags(out) + list(writes))

    def transp(self, out, in_, ident, reads=(), writes=()):
        self.op("tensor", lambda e: e.transpose(out, in_, ident),
                reads=_tags(in_, ident) + list(reads),
                writes=_tags(out) + list(writes))

    def act(self, out, in_, func, bias=0.0, scale=1.0, q="scalar", reads=(), writes=()):
        self.op(q, lambda e: e.activation(out, in_, func, bias=bias, scale=scale),
                reads=_tags(in_, bias, scale) + list(reads),
                writes=_tags(out) + list(writes))

    def tt(self, out, in0, in1, alu, q="vector", reads=(), writes=()):
        self.op(q, lambda e: e.tensor_tensor(out, in0, in1, alu),
                reads=_tags(in0, in1) + list(reads), writes=_tags(out) + list(writes))

    def ts(self, out, in0, s1, s2, op0, op1=ALU.bypass, q="vector", reads=(), writes=()):
        self.op(q, lambda e: e.tensor_scalar(out, in0, s1, s2, op0, op1),
                reads=_tags(in0, s1, s2) + list(reads),
                writes=_tags(out) + list(writes))

    def stt(self, out, in0, s, in1, op0, op1, q="vector", reads=(), writes=()):
        self.op(q, lambda e: e.scalar_tensor_tensor(out, in0, s, in1, op0, op1),
                reads=_tags(in0, in1, s) + list(reads), writes=_tags(out) + list(writes))

    def red(self, out, in_, alu, negate=None, absval=None, q="vector", reads=(), writes=()):
        self.op(q, lambda e: e.tensor_reduce(out, in_, AX.X, alu, negate=negate,
                                             apply_absolute_value=absval),
                reads=_tags(in_) + list(reads), writes=_tags(out) + list(writes))

    def copy(self, out, in_, q="vector", reads=(), writes=()):
        self.op(q, lambda e: e.tensor_copy(out, in_),
                reads=_tags(in_) + list(reads), writes=_tags(out) + list(writes))

    def recip(self, out, in_, q="vector", reads=(), writes=()):
        self.op(q, lambda e: e.reciprocal(out, in_),
                reads=_tags(in_) + list(reads), writes=_tags(out) + list(writes))

    def memset(self, ap, val, q="vector", reads=(), writes=()):
        self.op(q, lambda e: e.memset(ap, val),
                reads=list(reads), writes=_tags(ap) + list(writes))

    def scan_tt(self, out, d0, d1, initial, op0, op1, q="vector", reads=(), writes=()):
        self.op(q, lambda e: e.tensor_tensor_scan(out, d0, d1, initial, op0, op1),
                reads=_tags(d0, d1, initial) + list(reads),
                writes=_tags(out) + list(writes))

    def coll(self, kind, alu, ins, outs, reads, writes):
        self.op("pool", lambda e: e.collective_compute(
            kind, alu, replica_groups=[list(range(8))], ins=ins, outs=outs),
            reads=list(reads), writes=list(writes), drain_ok=False)

    def loadreg(self, key, ap, q="qs"):
        def fn(e):
            self.regs[key] = e.value_load(ap)
            return None
        self.op(q, fn, reads=_tags(ap), writes=(), inc_n=0)

    def barrier(self):
        snap = dict(self.count)
        for q in ALL_QUEUES:
            waits = sorted((q2, v) for q2, v in snap.items() if v > 0)
            inc_n = 16 if q in DMA_QUEUES else 1
            self.count[q] += inc_n
            if q in DMA_QUEUES:
                def fn(e):
                    return e.dma_start(out=self.bar_sb[0:1, 0:1],
                                       in_=self.bar_dr[0:1, 0:1])
            else:
                def fn(e):
                    return e.nop()
            self.ops[q].append((waits, fn, q, self.gidx, inc_n, True))
            self.gidx += 1
        for t in list(self.last_write):
            self.last_write[t] = ("tensor", self.count["tensor"])
        self.readers = {}



class Phase:

    def __init__(self, P=None, name="", ext=None):
        if P is None:
            self.nc = bass.Bass("TRN2", target_bir_lowering=False, debug=False)
            self.p = Prog(self.nc)
            self.merged = False
        else:
            self.nc = P.nc
            self.p = P
            self.merged = True
        self.name = name
        self.ext = ext or {}
        self.drams = {}

    def din(self, nm, shape, dt):
        if nm in self.ext:
            return self.ext[nm]
        ap = self.nc.dram_tensor(f"{self.name}_{nm}" if self.merged else nm,
                                 shape, dt, kind="ExternalInput").ap()
        self.drams[nm] = ap
        return ap

    def dio(self, nm, shape, dt, kind):
        if nm in self.ext:
            self.drams[nm] = self.ext[nm]
            return self.ext[nm]
        if self.merged and kind == "ExternalInput":
            raise RuntimeError(f"merged phase {self.name}: unwired input {nm}")
        k = "Internal" if self.merged else kind
        ap = self.nc.dram_tensor(f"{self.name}_{nm}" if self.merged else nm,
                                 shape, dt, kind=k).ap()
        self.drams[nm] = ap
        return ap

    def dout(self, nm, shape, dt):
        ap = self.nc.dram_tensor(f"{self.name}_{nm}" if self.merged else nm,
                                 shape, dt, kind="ExternalOutput").ap()
        self.drams[nm] = ap
        return ap

    def begin(self):
        if self.merged:
            self.p.open_scope()
            self.p.pfx = self.name + "_"
        return self.p

    def end(self):
        if self.merged:
            self.p.close_scope()
            self.p.pfx = ""
            self.p.barrier()
        else:
            self.p.emit()
        return self.nc


def emit_ln(p, xb, xn, T, onesb, st0_ps, st1_ps, bc_ps, rows, rowsb, sq, bcb, eps=1e-5):
    onescol = onesb[:, 0:1]
    onesrow = onesb[0:1, :]
    inv_d = 1.0 / D
    mean, ex2, scr = rows[:, 0:512], rows[:, 512:1024], rows[:, 1024:1536]
    rstdb, nmrb = rowsb[:, 0:512], rowsb[:, 512:1024]
    for t0 in range(0, T, 512):
        for c in range(NCH):
            xs = xb[:, c * T + t0: c * T + t0 + 512]
            p.tt(sq[:, :], xs, xs, ALU.mult)
            p.mm(st0_ps[0:1, :], onescol, xs, start=(c == 0), stop=(c == NCH - 1))
            p.mm(st1_ps[0:1, :], onescol, sq[:, :], start=(c == 0), stop=(c == NCH - 1))
        p.act(mean, st0_ps[0:1, :], AF.Copy, scale=inv_d)
        p.act(ex2, st1_ps[0:1, :], AF.Copy, scale=inv_d)
        p.tt(scr, mean, mean, ALU.mult)
        p.tt(scr, ex2, scr, ALU.subtract)
        p.ts(scr, scr, eps, None, ALU.add)
        p.act(scr, scr, AF.Sqrt)
        p.recip(scr, scr)
        p.copy(rstdb, scr)
        p.tt(scr, scr, mean, ALU.mult)
        p.ts(nmrb, scr, -1.0, None, ALU.mult)
        p.mm(bc_ps[:, 0:512], onesrow, rstdb)
        p.mm(bc_ps[:, 512:1024], onesrow, nmrb)
        p.copy(bcb[:, 0:512], bc_ps[:, 0:512])
        p.copy(bcb[:, 512:1024], bc_ps[:, 512:1024])
        for c in range(NCH):
            xs = xb[:, c * T + t0: c * T + t0 + 512]
            xo = xn[:, c * T + t0: c * T + t0 + 512]
            p.tt(xo, xs, bcb[:, 0:512], ALU.mult)
            p.tt(xo, xo, bcb[:, 512:1024], ALU.add)


def emit_ln32(p, xf, xn, T, onesf, st0_ps, st1_ps, bc_ps, rows, sqf, bcf, eps=1e-5):
    onescol = onesf[:, 0:1]
    onesrow = onesf[0:1, :]
    inv_d = 1.0 / D
    mean, ex2, scr = rows[:, 0:512], rows[:, 512:1024], rows[:, 1024:1536]
    for t0 in range(0, T, 512):
        for c in range(NCH):
            xs = xf[:, c * T + t0: c * T + t0 + 512]
            p.tt(sqf[:, :], xs, xs, ALU.mult)
            p.mm(st0_ps[0:1, :], onescol, xs, start=(c == 0), stop=(c == NCH - 1))
            p.mm(st1_ps[0:1, :], onescol, sqf[:, :], start=(c == 0), stop=(c == NCH - 1))
        p.act(mean, st0_ps[0:1, :], AF.Copy, scale=inv_d)
        p.act(ex2, st1_ps[0:1, :], AF.Copy, scale=inv_d)
        p.tt(scr, mean, mean, ALU.mult)
        p.tt(scr, ex2, scr, ALU.subtract)
        p.ts(scr, scr, eps, None, ALU.add)
        p.act(scr, scr, AF.Sqrt)
        p.recip(scr, scr)
        p.mm(bc_ps[:, 0:512], onesrow, scr)
        p.tt(scr, scr, mean, ALU.mult)
        p.ts(scr, scr, -1.0, None, ALU.mult)
        p.mm(bc_ps[:, 512:1024], onesrow, scr)
        p.copy(bcf[:, 0:512], bc_ps[:, 0:512])
        p.copy(bcf[:, 512:1024], bc_ps[:, 512:1024])
        for c in range(NCH):
            xs = xf[:, c * T + t0: c * T + t0 + 512]
            xo = xn[:, c * T + t0: c * T + t0 + 512]
            p.tt(xo, xs, bcf[:, 0:512], ALU.mult)
            p.tt(xo, xo, bcf[:, 512:1024], ALU.add)


class LNAcc:

    def __init__(self, p, name, T, onesf, ps_r0, ps_r1, ps_bc):
        self.p, self.T = p, T
        self.onesf = onesf
        self.ps_r0, self.ps_r1, self.ps_bc = ps_r0, ps_r1, ps_bc
        self.sums = p.sb(f"{name}_sums", [1, 2 * T], F32)
        self.sqf = p.sb(f"{name}_sqf", [128, 512], F32)
        self.bcf = p.sb(f"{name}_bcf", [128, 1024], F32)
        self.scr = p.sb(f"{name}_scr", [1, 512], F32)
        p.memset(self.sums[:, :], 0.0)

    def acc(self, x_tile, t0, w=512):
        p = self.p
        p.mm(self.ps_r0[0:1, 0:w], self.onesf[:, 0:1], x_tile)
        p.tt(self.sums[0:1, t0:t0 + w], self.sums[0:1, t0:t0 + w],
             self.ps_r0[0:1, 0:w], ALU.add)
        p.tt(self.sqf[:, 0:w], x_tile, x_tile, ALU.mult)
        p.mm(self.ps_r1[0:1, 0:w], self.onesf[:, 0:1], self.sqf[:, 0:w])
        p.tt(self.sums[0:1, self.T + t0:self.T + t0 + w],
             self.sums[0:1, self.T + t0:self.T + t0 + w], self.ps_r1[0:1, 0:w], ALU.add)

    def apply(self, x_dram, xn, xtile, eps=1e-5):
        p, T = self.p, self.T
        inv_d = 1.0 / D
        for t0 in range(0, T, 512):
            sx = self.sums[0:1, t0:t0 + 512]
            sq = self.sums[0:1, T + t0:T + t0 + 512]
            p.ts(sx, sx, inv_d, None, ALU.mult)
            p.tt(self.scr[0:1, :], sx, sx, ALU.mult)
            p.ts(self.scr[0:1, :], self.scr[0:1, :], -1.0, None, ALU.mult)
            p.stt(self.scr[0:1, :], sq, inv_d, self.scr[0:1, :], ALU.mult, ALU.add)
            p.ts(self.scr[0:1, :], self.scr[0:1, :], eps, None, ALU.add)
            p.act(self.scr[0:1, :], self.scr[0:1, :], AF.Sqrt)
            p.recip(self.scr[0:1, :], self.scr[0:1, :])
            p.mm(self.ps_bc[:, 0:512], self.onesf[0:1, :], self.scr[0:1, :])
            p.copy(self.bcf[:, 0:512], self.ps_bc[:, 0:512])
            p.tt(self.scr[0:1, :], self.scr[0:1, :], sx, ALU.mult)
            p.ts(self.scr[0:1, :], self.scr[0:1, :], -1.0, None, ALU.mult)
            p.mm(self.ps_bc[:, 0:512], self.onesf[0:1, :], self.scr[0:1, :])
            p.copy(self.bcf[:, 512:1024], self.ps_bc[:, 0:512])
            for c in range(NCH):
                p.dma("qs", xtile[:, :], x_dram[c * 128:(c + 1) * 128, t0:t0 + 512])
                p.tt(xtile[:, :], xtile[:, :], self.bcf[:, 0:512], ALU.mult)
                p.tt(xn[:, c * T + t0: c * T + t0 + 512], xtile[:, :],
                     self.bcf[:, 512:1024], ALU.add)


def load_weight_fm(p, wsb, wdram, nf, nch=NCH, q="qg"):
    for c in range(nch):
        p.dma(q, wsb[:, c * nf:(c + 1) * nf], wdram[c * 128:(c + 1) * 128, :])


def tm_proj(p, out_ps, xn, W, T, t0, nf0, nf, nch=NCH, extra_lhsT=None, start=True, stop=True):
    raise NotImplementedError



def build_prep0(P=None, ext=None):
    ph = Phase(P, "p0", ext)
    nc = ph.nc
    T = TPC
    xd = ph.din("x", [D, T], F32)
    wq_d = ph.din("wq", [D, 512], F32)
    wk_d = ph.din("wk", [D, 512], F32)
    wv_d = ph.din("wv", [D, 1024], F32)
    wb_d = ph.din("wb", [D, 4], F32)
    wg_d = ph.din("wg", [D, 1024], F32)
    ones_d = ph.din("ones", [128, 128], F32)
    qs_d = ph.dio("qs", [T, 512], BF16, "ExternalOutput")
    kn_d = ph.dio("kn", [T, 512], BF16, "ExternalOutput")
    kb_d = ph.dio("kb", [T, 512], BF16, "ExternalOutput")
    vb_d = ph.dio("vb", [T, 1024], BF16, "ExternalOutput")
    sg_d = ph.dio("sg", [T, 1024], BF16, "ExternalOutput")

    p = ph.begin()
    xf = p.sb("xf", [128, NCH * T], F32)
    xn = p.sb("xn", [128, NCH * T], BF16)
    onesb = p.sb("onesb", [128, 128], BF16)
    onesf = p.sb("onesf", [128, 128], F32)
    sqf = p.sb("sqf", [128, 512], F32)
    bcf = p.sb("bcf", [128, 1024], F32)
    wq = p.sb("wq_s", [128, NCH * 512], BF16)
    wk = p.sb("wk_s", [128, NCH * 512], BF16)
    wv = p.sb("wv_s", [128, NCH * 1024], BF16)
    wb = p.sb("wb_s", [128, NCH * 4], BF16)
    wg = p.sb("wg_s", [128, NCH * 1024], BF16)
    rows = p.sb("rows", [1, 3 * 512], F32)
    rowsb = p.sb("rowsb", [1, 2 * 512], BF16)
    sq = p.sb("sq", [128, 512], BF16)
    bcb = p.sb("bcb", [128, 1024], BF16)
    t_q = p.sb("t_q", [128, 512], BF16)
    t_k = p.sb("t_k", [128, 512], BF16)
    t_kf = p.sb("t_kf", [128, 512], F32)
    t_kb = p.sb("t_kb", [128, 512], BF16)
    t_v = p.sb("t_v", [128, 1024], BF16)
    t_g = p.sb("t_g", [128, 1024], BF16)
    t_ksq = p.sb("t_ksq", [128, 512], F32)
    t_beta = p.sb("t_beta", [128, 4], F32)
    t_rk = p.sb("t_rk", [128, 4], F32)

    ps_q = p.ps("ps_q", [128, 512])
    ps_k = p.ps("ps_k", [128, 512])
    ps_v = p.ps("ps_v", [128, 1024])
    ps_g = p.ps("ps_g", [128, 1024])
    ps_s = p.ps("ps_s", [128, 4])
    st0_ps = ps_q[0:1, :]
    st1_ps = ps_k[0:1, :]
    bc_ps = ps_v[:, 0:1024]

    for c in range(NCH):
        p.dma("qs", xf[:, c * T:(c + 1) * T], xd[c * 128:(c + 1) * 128, :])
    p.dma("qg", onesb[:, :], ones_d)
    p.dma("qs", onesf[:, :], ones_d)
    load_weight_fm(p, wq, wq_d, 512)
    load_weight_fm(p, wk, wk_d, 512)
    load_weight_fm(p, wv, wv_d, 1024)
    load_weight_fm(p, wb, wb_d, 4)
    load_weight_fm(p, wg, wg_d, 1024)

    emit_ln32(p, xf, xn, T, onesf, st0_ps, st1_ps, bc_ps, rows, sqf, bcf)

    SCALE_Q = float(128.0 ** -0.5)
    for t0 in range(0, T, 128):
        for c in range(NCH):
            lhs = xn[:, c * T + t0: c * T + t0 + 128]
            st, sp = (c == 0), (c == NCH - 1)
            p.mm(ps_q[:, :], lhs, wq[:, c * 512:(c + 1) * 512], start=st, stop=sp)
            p.mm(ps_k[:, :], lhs, wk[:, c * 512:(c + 1) * 512], start=st, stop=sp)
            p.mm(ps_v[:, 0:512], lhs, wv[:, c * 1024: c * 1024 + 512], start=st, stop=sp)
            p.mm(ps_v[:, 512:1024], lhs, wv[:, c * 1024 + 512:(c + 1) * 1024], start=st, stop=sp)
            p.mm(ps_s[:, :], lhs, wb[:, c * 4:(c + 1) * 4], start=st, stop=sp)
            p.mm(ps_g[:, 0:512], lhs, wg[:, c * 1024: c * 1024 + 512], start=st, stop=sp)
            p.mm(ps_g[:, 512:1024], lhs, wg[:, c * 1024 + 512:(c + 1) * 1024], start=st, stop=sp)
        p.act(t_q[:, :], ps_q[:, :], AF.Copy, scale=SCALE_Q)
        p.act(t_beta[:, :], ps_s[:, :], AF.Sigmoid)
        p.copy(t_kf[:, :], ps_k[:, :])
        p.tt(t_ksq[:, :], t_kf[:, :], t_kf[:, :], ALU.mult)
        for h in range(4):
            p.red(t_rk[:, h:h + 1], t_ksq[:, h * 128:(h + 1) * 128], ALU.add)
        p.ts(t_rk[:, :], t_rk[:, :], 1e-12, None, ALU.add)
        p.act(t_rk[:, :], t_rk[:, :], AF.Sqrt)
        p.recip(t_rk[:, :], t_rk[:, :])
        for h in range(4):
            p.ts(t_k[:, h * 128:(h + 1) * 128], t_kf[:, h * 128:(h + 1) * 128],
                 t_rk[:, h:h + 1], None, ALU.mult)
            p.ts(t_kb[:, h * 128:(h + 1) * 128], t_kf[:, h * 128:(h + 1) * 128],
                 t_rk[:, h:h + 1], t_beta[:, h:h + 1], ALU.mult, ALU.mult)
            p.ts(t_v[:, h * 256:(h + 1) * 256], ps_v[:, h * 256:(h + 1) * 256],
                 t_beta[:, h:h + 1], None, ALU.mult)
        p.act(t_g[:, 0:512], ps_g[:, 0:512], AF.Silu)
        p.act(t_g[:, 512:1024], ps_g[:, 512:1024], AF.Silu)
        p.dma("qs", qs_d[t0:t0 + 128, :], t_q[:, :])
        p.dma("qs", kn_d[t0:t0 + 128, :], t_k[:, :])
        p.dma("qs", kb_d[t0:t0 + 128, :], t_kb[:, :])
        p.dma("qs", vb_d[t0:t0 + 128, :], t_v[:, :])
        p.dma("qs", sg_d[t0:t0 + 128, :], t_g[:, :])

    ph.end()
    return ph



def build_delta(dk, dv, Lq=L, post="l0", layout="tm", P=None, name="dl", ext=None):
    ph = Phase(P, name, ext)
    nc = ph.nc
    NC_ = Lq // 128
    dkc = dk // 128
    dvc = dv // 128
    if layout == "tm":
        q_d = ph.dio("q", [Lq, dk], BF16, "ExternalInput")
        k_d = ph.dio("k", [Lq, dk], BF16, "ExternalInput")
        kb_d = ph.dio("kb", [Lq, dk], BF16, "ExternalInput")
        vb_d = ph.dio("vb", [Lq, dv], BF16, "ExternalInput")
    else:
        q_d = ph.dio("q", [dk, Lq], BF16, "ExternalInput")
        k_d = ph.dio("k", [dk, Lq], BF16, "ExternalInput")
        kb_d = ph.dio("kb", [dk, Lq], BF16, "ExternalInput")
        vb_d = ph.dio("vb", [dv, Lq], BF16, "ExternalInput")
    if post == "l0":
        sg_d = ph.dio("sg", [Lq, dv], BF16, "ExternalInput")
        nw_d = ph.din("nw", [1, dv], F32)
    id_d = ph.din("ident", [128, 128], F32)
    sl_d = ph.din("sl", [128, 128], F32)
    su_d = ph.din("su", [128, 128], F32)
    sui_d = ph.din("sui", [128, 128], F32)
    og_d = ph.dio("og", [Lq, dv], BF16, "ExternalOutput")

    p = ph.begin()
    qS = p.sb("qS", [128, Lq * dkc], BF16)
    kS = p.sb("kS", [128, Lq * dkc], BF16)
    kbS = p.sb("kbS", [128, Lq * dkc], BF16)
    vbS = p.sb("vbS", [128, Lq * dvc], BF16)
    ogS = p.sb("ogS", [128, Lq * dvc], BF16)
    if post == "l0":
        sgS = p.sb("sgS", [128, Lq * dvc], BF16)
        nwb = p.sb("nwb", [128, dv], BF16)
        nwrow = p.sb("nwrow", [1, dv], BF16)
    identb = p.sb("identb", [128, 128], BF16)
    slm = p.sb("slm", [128, 128], F32)
    sum_ = p.sb("sum_", [128, 128], F32)
    suim = p.sb("suim", [128, 128], BF16)
    onesb = p.sb("onesb2", [1, 128], BF16)
    w_qf = p.sb("w_qf", [128, dk], BF16)
    w_kf = p.sb("w_kf", [128, dk], BF16)
    w_kbf = p.sb("w_kbf", [128, dk], BF16)
    w_kt = p.sb("w_kt", [128, dk], BF16)
    w_kbt = p.sb("w_kbt", [128, dk], BF16)
    w_vbt = p.sb("w_vbt", [128, dv], BF16)
    Asb = p.sb("Asb", [128, 128], BF16)
    ATsb = p.sb("ATsb", [128, 128], BF16)
    Pa = p.sb("Pa", [128, 128], BF16)
    Pb = p.sb("Pb", [128, 128], BF16)
    PTa = p.sb("PTa", [128, 128], BF16)
    PTb = p.sb("PTb", [128, 128], BF16)
    TTf = p.sb("TTf", [128, 128], F32)
    TTb = p.sb("TTb", [128, 128], BF16)
    identf = p.sb("identf", [128, 128], F32)
    usb = p.sb("usb", [128, dv], BF16)
    u2sb = p.sb("u2sb", [128, dv], BF16)
    wfm = p.sb("wfm", [128, dk], BF16)
    atT = p.sb("atT", [128, 128], BF16)
    Sf = p.sb("Sf", [128, dkc * dv], F32)
    Sb = p.sb("Sb", [128, dkc * dv], BF16)
    osb = p.sb("osb", [128, dv], F32)
    osq = p.sb("osq", [128, dv], F32)
    orr = p.sb("orr", [128, 1], F32)
    gat = p.sb("gat", [128, dv], BF16)

    ps_tr = p.ps("ps_tr", [128, 128], BF16)
    ps_m0 = p.ps("ps_m0", [128, 128])
    ps_m1 = p.ps("ps_m1", [128, 128])
    ps_u = p.ps("ps_u", [128, dv])
    ps_o = p.ps("ps_o", [128, dv])

    def load_tm(sb_t, dr, nf):
        p.dma("qs", sb_t[:, :].rearrange("p (n f) -> p n f", f=nf),
              dr.rearrange("(n p) f -> p n f", p=128))

    def load_fm(sb_t, dr, nch):
        for c in range(nch):
            p.dma("qs", sb_t[:, c * Lq:(c + 1) * Lq], dr[c * 128:(c + 1) * 128, :])

    if layout == "tm":
        load_tm(qS, q_d, dk); load_tm(kS, k_d, dk)
        load_tm(kbS, kb_d, dk); load_tm(vbS, vb_d, dv)
    else:
        load_fm(qS, q_d, dkc); load_fm(kS, k_d, dkc)
        load_fm(kbS, kb_d, dkc); load_fm(vbS, vb_d, dvc)
    if post == "l0":
        load_tm(sgS, sg_d, dv)
        p.dma("qg", nwrow[:, :], nw_d)
    p.dma("qg", identb[:, :], id_d)
    p.dma("qs", identf[:, :], id_d)
    p.dma("qg", slm[:, :], sl_d)
    p.dma("qg", sum_[:, :], su_d)
    p.dma("qg", suim[:, :], sui_d)
    p.memset(onesb[:, :], 1.0)
    p.memset(Sf[:, :], 0.0)
    p.memset(Sb[:, :], 0.0)
    if post == "l0":
        p.mm(ps_o[:, 0:dv], onesb[:, :], nwrow[:, :])
        p.copy(nwb[:, :], ps_o[:, 0:dv])

    NSQ = 6
    for ci in range(NC_):
        if layout == "tm":
            qt = qS[:, ci * dk:(ci + 1) * dk]
            kt = kS[:, ci * dk:(ci + 1) * dk]
            kbt = kbS[:, ci * dk:(ci + 1) * dk]
            vbt = vbS[:, ci * dv:(ci + 1) * dv]
            qf, kf, kbf = w_qf, w_kf, w_kbf
            for d in range(dkc):
                for (src, dst) in ((qt, qf), (kt, kf), (kbt, kbf)):
                    p.transp(ps_tr[:, :], src[:, d * 128:(d + 1) * 128], identb[:, :])
                    p.copy(dst[:, d * 128:(d + 1) * 128], ps_tr[:, :])
        else:
            def fmt(sbuf_t, d):
                return sbuf_t[:, d * Lq + ci * 128: d * Lq + (ci + 1) * 128]
            qf = [fmt(qS, d) for d in range(dkc)]
            kf = [fmt(kS, d) for d in range(dkc)]
            kbf = [fmt(kbS, d) for d in range(dkc)]
            kt, kbt, vbt = w_kt, w_kbt, w_vbt
            for d in range(dkc):
                p.transp(ps_tr[:, :], kf[d], identb[:, :])
                p.copy(kt[:, d * 128:(d + 1) * 128], ps_tr[:, :])
                p.transp(ps_tr[:, :], kbf[d], identb[:, :])
                p.copy(kbt[:, d * 128:(d + 1) * 128], ps_tr[:, :])
            for d in range(dvc):
                p.transp(ps_tr[:, :], fmt(vbS, d), identb[:, :])
                p.copy(vbt[:, d * 128:(d + 1) * 128], ps_tr[:, :])

        def Qf(d):
            return qf[d] if layout == "fm" else qf[:, d * 128:(d + 1) * 128]

        def Kf(d):
            return kf[d] if layout == "fm" else kf[:, d * 128:(d + 1) * 128]

        def KBf(d):
            return kbf[d] if layout == "fm" else kbf[:, d * 128:(d + 1) * 128]

        for d in range(dkc):
            p.mm(ps_m0[:, :], KBf(d), Kf(d), start=(d == 0), stop=(d == dkc - 1))
            p.mm(ps_m1[:, :], Kf(d), KBf(d), start=(d == 0), stop=(d == dkc - 1))
        p.stt(Asb[:, :], ps_m0[:, :], -1.0, slm[:, :], ALU.mult, ALU.mult)
        p.stt(ATsb[:, :], ps_m1[:, :], -1.0, sum_[:, :], ALU.mult, ALU.mult)
        p.tt(TTf[:, :], ATsb[:, :], identf[:, :], ALU.add)
        p.copy(TTb[:, :], TTf[:, :])
        P, PT, Pn, PTn = Pa, PTa, Pb, PTb
        p.copy(P[:, :], Asb[:, :])
        p.copy(PT[:, :], ATsb[:, :])
        for j in range(NSQ):
            p.mm(ps_m0[:, :], PT[:, :], P[:, :])
            p.copy(Pn[:, :], ps_m0[:, :])
            if j < NSQ - 1:
                p.mm(ps_m1[:, :], P[:, :], PT[:, :])
                p.copy(PTn[:, :], ps_m1[:, :])
            p.mm(ps_m0[:, :], Pn[:, :], TTb[:, :])
            p.tt(TTf[:, :], TTf[:, :], ps_m0[:, :], ALU.add)
            p.copy(TTb[:, :], TTf[:, :])
            P, Pn = Pn, P
            PT, PTn = PTn, PT
        p.mm(ps_u[:, 0:dv], TTb[:, :], vbt[:, :])
        p.copy(usb[:, :], ps_u[:, 0:dv])
        for d in range(dkc):
            p.mm(ps_m0[:, :], kbt[:, d * 128:(d + 1) * 128], TTb[:, :])
            p.copy(wfm[:, d * 128:(d + 1) * 128], ps_m0[:, :])
        for d in range(dkc):
            p.mm(ps_u[:, 0:dv], wfm[:, d * 128:(d + 1) * 128],
                 Sb[:, d * dv:(d + 1) * dv], start=(d == 0), stop=(d == dkc - 1))
        p.tt(u2sb[:, :], usb[:, :], ps_u[:, 0:dv], ALU.subtract)
        for d in range(dkc):
            p.mm(ps_m1[:, :], Kf(d), Qf(d), start=(d == 0), stop=(d == dkc - 1))
        p.tt(atT[:, :], ps_m1[:, :], suim[:, :], ALU.mult)
        for d in range(dkc):
            p.mm(ps_o[:, 0:dv], Qf(d), Sb[:, d * dv:(d + 1) * dv],
                 start=(d == 0), stop=False)
        p.mm(ps_o[:, 0:dv], atT[:, :], u2sb[:, :], start=False, stop=True)
        for d in range(dkc):
            p.mm(ps_u[:, 0:dv], kt[:, d * 128:(d + 1) * 128], u2sb[:, :])
            p.tt(Sf[:, d * dv:(d + 1) * dv], Sf[:, d * dv:(d + 1) * dv],
                 ps_u[:, 0:dv], ALU.add)
            p.copy(Sb[:, d * dv:(d + 1) * dv], Sf[:, d * dv:(d + 1) * dv])
        ot = ogS[:, ci * dv:(ci + 1) * dv]
        if post == "l0":
            p.copy(osb[:, :], ps_o[:, 0:dv])
            p.tt(osq[:, :], osb[:, :], osb[:, :], ALU.mult)
            p.red(orr[:, :], osq[:, :], ALU.add)
            p.ts(orr[:, :], orr[:, :], 1.0 / dv, 1e-5, ALU.mult, ALU.add)
            p.act(orr[:, :], orr[:, :], AF.Sqrt)
            p.recip(orr[:, :], orr[:, :])
            sgt = sgS[:, ci * dv:(ci + 1) * dv]
            p.tt(gat[:, :], nwb[:, :], sgt[:, :], ALU.mult)
            p.stt(ot, osb[:, :], orr[:, 0:1], gat[:, :], ALU.mult, ALU.mult)
        else:
            p.copy(ot, ps_o[:, 0:dv])

    p.dma("qs", og_d.rearrange("(n p) f -> p n f", p=128),
          ogS[:, :].rearrange("p (n f) -> p n f", f=dv))
    ph.end()
    return ph



def build_wo_ffn0(P=None, ext=None):
    ph = Phase(P, "f0", ext)
    nc = ph.nc
    T = TPC
    x0_d = ph.din("x0", [D, T], F32)
    og_d = ph.dio("ogt", [T, D], BF16, "ExternalInput")
    wo_d = ph.din("wo", [D, D], F32)
    w1_d = ph.din("w1T", [D, 4096], F32)
    w2_d = ph.din("w2T", [D, 4096], F32)
    h1_d = ph.din("h1T", [4096, 4096], F32)
    h2_d = ph.din("h2T", [4096, 4096], F32)
    w3_d = ph.din("w3T", [4096, D], F32)
    s1_d = ph.din("s1", [4096, 1], F32)
    s2_d = ph.din("s2", [4096, 1], F32)
    sh1_d = ph.din("sh1", [4096, 1], F32)
    sh2_d = ph.din("sh2", [4096, 1], F32)
    s3_d = ph.din("s3", [D, 1], F32)
    ones_d = ph.din("ones", [128, 128], F32)
    x2_d = ph.dio("x2", [D, T], F32, "ExternalOutput")
    xn2_d = ph.dio("xn2o", [D, T], BF16, "ExternalOutput")

    p = ph.begin()
    x1f = p.sb("x1f", [128, NCH * T], F32)
    xn1 = p.sb("xn1", [128, NCH * T], BF16)
    ogf = p.sb("ogf_s", [128, NCH * T], BF16)
    onesf = p.sb("onesf", [128, 128], F32)
    rows = p.sb("rows", [1, 3 * 512], F32)
    sqf = p.sb("sqf", [128, 512], F32)
    bcf = p.sb("bcf", [128, 1024], F32)
    s1c = p.sb("s1c", [128, 32], F32)
    s2c = p.sb("s2c", [128, 32], F32)
    sh1c = p.sb("sh1c", [128, 32], F32)
    sh2c = p.sb("sh2c", [128, 32], F32)
    s3c = p.sb("s3c", [128, 8], F32)
    TT1 = 512
    hA = ogf
    hB = p.sb("hB", [128, 32 * TT1], BF16)
    wstrA = p.sb("wstrA", [128, 32 * 128], BF16)
    wstrB = p.sb("wstrB", [128, 32 * 128], BF16)
    tmp1 = p.sb("tmp1", [128, 512], BF16)

    ps_a = p.ps("ps_a", [128, 512])
    ps_b = p.ps("ps_b", [128, 512])
    ps_c = p.ps("ps_c", [128, 1024])

    for c in range(NCH):
        p.dma("qs", x1f[:, c * T:(c + 1) * T], x0_d[c * 128:(c + 1) * 128, :])
        p.dma("qs", ogf[:, c * T:(c + 1) * T], og_d[:, c * 128:(c + 1) * 128],
              transpose=True)
    p.dma("qs", onesf[:, :], ones_d)
    p.dma("qg", s1c[:, :], s1_d.rearrange("(c p) one -> p (c one)", p=128), allow_slow_non_contiguous=True)
    p.dma("qg", s2c[:, :], s2_d.rearrange("(c p) one -> p (c one)", p=128), allow_slow_non_contiguous=True)
    p.dma("qg", sh1c[:, :], sh1_d.rearrange("(c p) one -> p (c one)", p=128), allow_slow_non_contiguous=True)
    p.dma("qg", sh2c[:, :], sh2_d.rearrange("(c p) one -> p (c one)", p=128), allow_slow_non_contiguous=True)
    p.dma("qg", s3c[:, :], s3_d.rearrange("(c p) one -> p (c one)", p=128), allow_slow_non_contiguous=True)

    for oc in range(NCH):
        p.dma("qg", wstrA[:, 0:NCH * 128].rearrange("p (c n) -> p c n", n=128),
              wo_d[:, oc * 128:(oc + 1) * 128].rearrange("(c p) n -> p c n", p=128))
        for t0 in range(0, T, 512):
            for kc in range(NCH):
                p.mm(ps_a[:, :], wstrA[:, kc * 128:(kc + 1) * 128],
                     ogf[:, kc * T + t0: kc * T + t0 + 512],
                     start=(kc == 0), stop=(kc == NCH - 1))
            sl_ = slice(oc * T + t0, oc * T + t0 + 512)
            p.tt(x1f[:, sl_], x1f[:, sl_], ps_a[:, :], ALU.add)

    emit_ln32(p, x1f, xn1, T, onesf, ps_a[0:1, :], ps_b[0:1, :], ps_c[:, 0:1024],
              rows, sqf, bcf)

    for tb in range(0, T, TT1):
        for oc in range(32):
            p.dma("qg", wstrA[:, 0:NCH * 128].rearrange("p (c n) -> p c n", n=128),
                  w1_d[:, oc * 128:(oc + 1) * 128].rearrange("(c p) n -> p c n", p=128))
            p.dma("qg", wstrB[:, 0:NCH * 128].rearrange("p (c n) -> p c n", n=128),
                  w2_d[:, oc * 128:(oc + 1) * 128].rearrange("(c p) n -> p c n", p=128))
            for kc in range(NCH):
                p.mm(ps_a[:, :], wstrA[:, kc * 128:(kc + 1) * 128],
                     xn1[:, kc * T + tb: kc * T + tb + 512],
                     start=(kc == 0), stop=(kc == NCH - 1))
                p.mm(ps_b[:, :], wstrB[:, kc * 128:(kc + 1) * 128],
                     xn1[:, kc * T + tb: kc * T + tb + 512],
                     start=(kc == 0), stop=(kc == NCH - 1))
            p.act(tmp1[:, :], ps_a[:, :], AF.Silu, scale=s1c[:, oc:oc + 1])
            p.stt(hA[:, oc * TT1: oc * TT1 + 512], ps_b[:, :],
                  s2c[:, oc:oc + 1], tmp1[:, :], ALU.mult, ALU.mult)
        for (hsrc, hdst, w_d_, s_c) in ((hA, hB, h1_d, sh1c), (hB, hA, h2_d, sh2c)):
            for oc in range(32):
                wbuf = wstrA if oc % 2 == 0 else wstrB
                p.dma("qg", wbuf[:, :].rearrange("p (c n) -> p c n", n=128),
                      w_d_[:, oc * 128:(oc + 1) * 128].rearrange("(c p) n -> p c n", p=128))
                for kc in range(32):
                    p.mm(ps_a[:, :], wbuf[:, kc * 128:(kc + 1) * 128],
                         hsrc[:, kc * TT1: kc * TT1 + 512],
                         start=(kc == 0), stop=(kc == 31))
                p.act(hdst[:, oc * TT1: oc * TT1 + 512], ps_a[:, :],
                      AF.Gelu, scale=s_c[:, oc:oc + 1])
        for oc in range(NCH):
            wbuf = wstrA if oc % 2 == 0 else wstrB
            p.dma("qg", wbuf[:, :].rearrange("p (c n) -> p c n", n=128),
                  w3_d[:, oc * 128:(oc + 1) * 128].rearrange("(c p) n -> p c n", p=128))
            for kc in range(32):
                p.mm(ps_b[:, :], wbuf[:, kc * 128:(kc + 1) * 128],
                     hA[:, kc * TT1: kc * TT1 + 512],
                     start=(kc == 0), stop=(kc == 31))
            sl_ = slice(oc * T + tb, oc * T + tb + 512)
            p.stt(x1f[:, sl_], ps_b[:, :], s3c[:, oc:oc + 1], x1f[:, sl_],
                  ALU.mult, ALU.add)

    for c in range(NCH):
        p.dma("qs", x2_d[c * 128:(c + 1) * 128, :], x1f[:, c * T:(c + 1) * T])
    xn2b = hB
    emit_ln32(p, x1f, xn2b, T, onesf, ps_a[0:1, :], ps_b[0:1, :], ps_c[:, 0:1024],
              rows, sqf, bcf)
    for c in range(NCH):
        p.dma("qs", xn2_d[c * 128:(c + 1) * 128, :], xn2b[:, c * T:(c + 1) * T])
    ph.end()
    return ph





def build_prep1(P=None, ext=None):
    ph = Phase(P, "p1", ext)
    nc = ph.nc
    T2 = 2560
    HLO = 512
    TV = 2048
    xh_d = ph.dio("xnh", [D, T2], BF16, "ExternalInput")
    hmask_d = ph.din("hmask", [128, 1], F32)
    wq_d = ph.din("wq", [D, D], F32)
    wk_d = ph.din("wk", [D, D], F32)
    wv_d = ph.din("wv", [D, D], F32)
    wb_d = ph.din("wb", [D, 4], F32)
    cq_d = ph.din("cq", [D, 4], F32)
    ck_d = ph.din("ck", [D, 4], F32)
    cv_d = ph.din("cv", [D, 4], F32)
    f5_d = ph.din("f5", [D, 5], F32)
    f64_d = ph.din("f64", [D, 64], F32)
    ones_d = ph.din("ones", [128, 128], F32)
    id_d = ph.din("ident", [128, 128], F32)
    outs = {}
    for nm in ("q1", "k1", "kb1", "vb1", "vd", "ls", "ll"):
        outs[nm] = ph.dio(nm, [D, TV], BF16, "ExternalOutput")
    beta_d = ph.dio("beta", [4, TV], F32, "ExternalOutput")

    p = ph.begin()
    xn = p.sb("xn", [128, NCH * T2], BF16)
    lin = p.sb("lin", [128, NCH * T2], BF16)
    cvb = p.sb("cvb", [128, NCH * T2], BF16)
    wbuf = p.sb("wbuf", [128, NCH * 128], BF16)
    onesb = p.sb("onesb", [128, 128], BF16)
    rows = p.sb("rows", [1, 3 * 512], F32)
    rowsb = p.sb("rowsb", [1, 2 * 512], BF16)
    sq = p.sb("sq", [128, 512], BF16)
    bcb = p.sb("bcb", [128, 1024], BF16)
    betas = p.sb("betas", [4, TV], F32)
    betasb = p.sb("betasb", [4, TV], BF16)
    betab = p.sb("betab", [1, 4 * TV], BF16)
    id4b = p.sb("id4b", [4, 4], BF16)
    tq = p.sb("tq", [128, 8 * 4], F32)
    tk = p.sb("tk", [128, 8 * 4], F32)
    tv = p.sb("tv", [128, 8 * 4], F32)
    t5 = p.sb("t5", [128, 8 * 5], F32)
    t64 = p.sb("t64", [128, 8 * 64], F32)
    rowq = p.sb("rowq", [1, 512], F32)
    rowqb = p.sb("rowqb", [1, 512], BF16)

    ps_a = p.ps("ps_a", [128, 512])
    ps_bc = p.ps("ps_bc", [128, 512])
    ps_ln = p.ps("ps_ln", [128, 1024])
    ps_be = p.ps("ps_be", [4, 512])

    acc32 = p.sb("acc32", [128, T2], F32)
    acc32b = p.sb("acc32b", [128, 2048], F32)

    hm = p.sb("hm", [128, 1], F32)
    p.dma("qg", hm[:, :], hmask_d)
    for c in range(NCH):
        p.dma("qs", xn[:, c * T2:(c + 1) * T2], xh_d[c * 128:(c + 1) * 128, :])
        p.ts(xn[:, c * T2:c * T2 + HLO], xn[:, c * T2:c * T2 + HLO],
             hm[:, 0:1], None, ALU.mult)
    p.dma("qg", onesb[:, :], ones_d)
    p.dma("qg", id4b[:, :], id_d[0:4, 0:4])
    for tb, td, K in ((tq, cq_d, 4), (tk, ck_d, 4), (tv, cv_d, 4), (t5, f5_d, 5), (t64, f64_d, 64)):
        p.dma("qs", tb[:, :].rearrange("p (c k) -> p c k", k=K),
              td.rearrange("(c p) k -> p c k", p=128))


    wbb = p.sb("wbb", [128, NCH * 4], BF16)
    load_weight_fm(p, wbb, wb_d, 4)
    for t0 in range(HLO, T2, 512):
        for kc in range(NCH):
            p.mm(ps_be[:, :], wbb[:, kc * 4:(kc + 1) * 4], xn[:, kc * T2 + t0: kc * T2 + t0 + 512],
                 start=(kc == 0), stop=(kc == NCH - 1))
        p.act(betas[:, t0 - HLO:t0 - HLO + 512], ps_be[:, :], AF.Sigmoid)
    p.copy(betasb[:, :], betas[:, :])
    for h in range(4):
        for t0 in range(0, TV, 512):
            p.mm(ps_bc[0:1, :], id4b[:, h:h + 1], betasb[:, t0:t0 + 512])
            p.copy(betab[0:1, h * TV + t0: h * TV + t0 + 512], ps_bc[0:1, :])
    p.dma("qs", beta_d, betas[:, :])

    def conv_free(dst, dst_off, src, src_off, width, taps, c, K, q="vector", acc_f32=None):
        for j in range(K):
            so = src_off - (K - 1) + j
            if j == 0:
                p.ts(dst[:, dst_off:dst_off + width], src[:, so:so + width],
                     taps[:, c * K:c * K + 1], None, ALU.mult, q=q)
            else:
                p.stt(dst[:, dst_off:dst_off + width], src[:, so:so + width],
                      taps[:, c * K + j:c * K + j + 1], dst[:, dst_off:dst_off + width],
                      ALU.mult, ALU.add, q=q)

    def proj_conv(w_d_, taps, start_t):
        for oc in range(NCH):
            p.dma("qg", wbuf[:, :].rearrange("p (c n) -> p c n", n=128),
                  w_d_[:, oc * 128:(oc + 1) * 128].rearrange("(c p) n -> p c n", p=128))
            for t0 in range(start_t, T2, 512):
                for kc in range(NCH):
                    p.mm(ps_a[:, :], wbuf[:, kc * 128:(kc + 1) * 128],
                         xn[:, kc * T2 + t0: kc * T2 + t0 + 512],
                         start=(kc == 0), stop=(kc == NCH - 1))
                p.copy(lin[:, oc * T2 + t0: oc * T2 + t0 + 512], ps_a[:, :])
            co = oc * T2 + start_t + 128
            w_ = T2 - start_t - 128
            conv_free(acc32, start_t + 128, lin, co, w_, taps, oc, 4)
            p.act(lin[:, co:co + w_], acc32[:, start_t + 128:start_t + 128 + w_], AF.Silu)
        return lin

    proj_conv(wq_d, tq, HLO - 512 if False else 0)
    def l2_out(out_name, eps):
        for h in range(4):
            c0, c1 = 2 * h, 2 * h + 1
            for t0 in range(HLO, T2, 512):
                for ci, cc in enumerate((c0, c1)):
                    sl_ = slice(cc * T2 + t0, cc * T2 + t0 + 512)
                    p.tt(sq[:, :], lin[:, sl_], lin[:, sl_], ALU.mult)
                    p.mm(ps_bc[0:1, :], onesb[:, 0:1], sq[:, :], start=(ci == 0), stop=(ci == 1))
                p.act(rowq[:, :], ps_bc[0:1, :], AF.Copy)
                p.ts(rowq[:, :], rowq[:, :], eps, None, ALU.add)
                p.act(rowq[:, :], rowq[:, :], AF.Sqrt)
                p.recip(rowq[:, :], rowq[:, :])
                p.copy(rowqb[:, :], rowq[:, :])
                p.mm(ps_bc[:, :], onesb[0:1, :], rowqb[:, :])
                p.copy(bcb[:, 0:512], ps_bc[:, :])
                for cc in (c0, c1):
                    sl_ = slice(cc * T2 + t0, cc * T2 + t0 + 512)
                    p.tt(cvb[:, sl_], lin[:, sl_], bcb[:, 0:512], ALU.mult)
        for c in range(NCH):
            p.dma("qs", outs[out_name][c * 128:(c + 1) * 128, :],
                  cvb[:, c * T2 + HLO:(c + 1) * T2])
    l2_out("q1", 1e-6)

    proj_conv(wk_d, tk, 0)
    l2_out("k1", 1e-6)
    for h in range(4):
        for cc in (2 * h, 2 * h + 1):
            for t0 in range(HLO, T2, 512):
                p.mm(ps_bc[:, :], onesb[0:1, :],
                     betab[0:1, h * TV + t0 - HLO: h * TV + t0 - HLO + 512])
                p.copy(bcb[:, 0:512], ps_bc[:, :])
                sl_ = slice(cc * T2 + t0, cc * T2 + t0 + 512)
                p.tt(lin[:, sl_], cvb[:, sl_], bcb[:, 0:512], ALU.mult)
    for c in range(NCH):
        p.dma("qs", outs["kb1"][c * 128:(c + 1) * 128, :], lin[:, c * T2 + HLO:(c + 1) * T2])

    proj_conv(wv_d, tv, 0)
    for c in range(NCH):
        p.dma("qs", outs["vd"][c * 128:(c + 1) * 128, :], lin[:, c * T2 + HLO:(c + 1) * T2])
    for h in range(4):
        for cc in (2 * h, 2 * h + 1):
            for t0 in range(HLO, T2, 512):
                p.mm(ps_bc[:, :], onesb[0:1, :],
                     betab[0:1, h * TV + t0 - HLO: h * TV + t0 - HLO + 512])
                p.copy(bcb[:, 0:512], ps_bc[:, :])
                sl_ = slice(cc * T2 + t0, cc * T2 + t0 + 512)
                p.tt(cvb[:, sl_], lin[:, sl_], bcb[:, 0:512], ALU.mult)
    for c in range(NCH):
        p.dma("qs", outs["vb1"][c * 128:(c + 1) * 128, :], cvb[:, c * T2 + HLO:(c + 1) * T2])

    for c in range(NCH):
        conv_free(acc32b, 0, lin, c * T2 + HLO, TV, t5, c, 5)
        p.copy(cvb[:, c * T2 + HLO:(c + 1) * T2], acc32b[:, :])
        p.dma("qs", outs["ls"][c * 128:(c + 1) * 128, :], cvb[:, c * T2 + HLO:(c + 1) * T2])
    for c in range(NCH):
        o = c * T2 + HLO
        for j in range(64):
            so = o - 63 + j
            if j == 0:
                p.ts(acc32b[:, :], lin[:, so:so + TV], t64[:, c * 64:c * 64 + 1],
                     None, ALU.mult)
            else:
                p.stt(acc32b[:, :], lin[:, so:so + TV], t64[:, c * 64 + j:c * 64 + j + 1],
                      acc32b[:, :], ALU.mult, ALU.add)
        p.copy(cvb[:, o:o + TV], acc32b[:, :])
        p.dma("qs", outs["ll"][c * 128:(c + 1) * 128, :], cvb[:, o:o + TV])

    ph.end()
    return ph



def build_mix1(P=None, ext=None, dbg=False):
    ph = Phase(P, "m1", ext)
    nc = ph.nc
    T = TPC
    x2_d = ph.dio("x2", [D, T], F32, "ExternalInput")
    xn2_d = ph.dio("xn2", [D, T], BF16, "ExternalInput")
    ls_d = ph.dio("lsf", [D, T], BF16, "ExternalInput")
    ll_d = ph.dio("llf", [D, T], BF16, "ExternalInput")
    vd_d = ph.dio("vdf", [D, T], BF16, "ExternalInput")
    del_d = ph.dio("delt", [T, D], BF16, "ExternalInput")
    wg1x_d = ph.din("wg1x", [D, D], F32)
    wg1s_d = ph.din("wg1s", [16, D], F32)
    bg1_d = ph.din("bg1", [D, 1], F32)
    wg2_d = ph.din("wg2", [D, 4], F32)
    bg2_d = ph.din("bg2", [1, 4], F32)
    lt_d = ph.din("ltemp", [1, 1], F32)
    crl_d = ph.din("crl", [1, 4], F32)
    wres_d = ph.din("wres", [D, 4], F32)
    bres_d = ph.din("bres", [1, 4], F32)
    wog1x_d = ph.din("wog1x", [D, 512], F32)
    wog1s_d = ph.din("wog1s", [8, 512], F32)
    bog1_d = ph.din("bog1", [512, 1], F32)
    wog2_d = ph.din("wog2", [512, 1], F32)
    bog2_d = ph.din("bog2", [1, 1], F32)
    nw4_d = ph.din("nw4", [1, D], F32)
    wo_d = ph.din("wo", [D, D], F32)
    ones_d = ph.din("ones", [128, 128], F32)
    id_d = ph.din("ident", [128, 128], F32)
    x3_d = ph.dio("x3", [D, T], F32, "ExternalOutput")
    xn3_d = ph.dio("xn3", [D, T], BF16, "ExternalOutput")
    dbg_d = ph.dio("m1dbg", [128, 4 * D], BF16, "ExternalOutput") if dbg else None

    p = ph.begin()
    xn = p.sb("xn_s", [128, NCH * T], BF16)
    OFM = p.sb("OFM", [128, NCH * T], BF16)
    xn3 = p.sb("xn3_s", [128, NCH * T], BF16)
    identb = p.sb("identb", [128, 128], BF16)
    onesb = p.sb("onesb", [128, 128], BF16)
    onesf = p.sb("onesf", [128, 128], F32)
    wg1x = p.sb("wg1x_s", [128, NCH * D], BF16)
    wg1s = p.sb("wg1s_s", [16, D], BF16)
    wg2 = p.sb("wg2_s", [128, NCH * 4], BF16)
    wres = p.sb("wres_s", [128, NCH * 4], BF16)
    wog1s = p.sb("wog1s_s", [8, 512], BF16)
    wog2 = p.sb("wog2_s", [128, 4], BF16)
    wbuf = p.sb("wbuf", [128, NCH * 128], BF16)
    bg1c = p.sb("bg1c", [128, 8], F32)
    bog1c = p.sb("bog1c", [128, 4], F32)
    bg2r = p.sb("bg2r", [1, 4], BF16)
    bresr = p.sb("bresr", [1, 4], BF16)
    bog2r = p.sb("bog2r", [1, 1], BF16)
    ltr = p.sb("ltr", [1, 4], F32)
    spcol = p.sb("spcol", [128, 1], F32)
    staticb = p.sb("staticb", [128, 4], F32)
    nwb = p.sb("nwb", [128, D], BF16)
    nwr = p.sb("nwr", [1, D], BF16)
    tdel = p.sb("tdel", [128, D], BF16)
    tls = p.sb("tls", [128, D], BF16)
    tll = p.sb("tll", [128, D], BF16)
    tvd = p.sb("tvd", [128, D], BF16)
    sqb = p.sb("sqb", [128, D], F32)
    s4a = p.sb("s4a", [128, 4], F32)
    s4b = p.sb("s4b", [128, 4], F32)
    s4c = p.sb("s4c", [128, 4], F32)
    s4d = p.sb("s4d", [128, 4], F32)
    st16 = p.sb("st16", [128, 16], F32)
    st16b = p.sb("st16b", [128, 16], BF16)
    stfm = p.sb("stfm", [16, 128], BF16)
    og8 = p.sb("og8", [128, 8], F32)
    og8b = p.sb("og8b", [128, 8], BF16)
    ogfm = p.sb("ogfm", [8, 128], BF16)
    gg = p.sb("gg", [128, NCH * 128], BF16)
    ogb = p.sb("ogb", [128, 4 * 128], BF16)
    pvec = p.sb("pvec", [128, 4], F32)
    rg = p.sb("rg", [128, 4], F32)
    ogc = p.sb("ogc", [128, 1], F32)
    obuf = p.sb("obuf", [128, D], F32)
    ob16 = p.sb("ob16", [128, D], BF16)
    rmsc = p.sb("rmsc", [128, 4], F32)
    x2t = p.sb("x2t", [128, 512], F32)
    x3s = p.sb("x3s", [128, 512], F32)

    ps_t = p.ps("ps_t", [128, 128], BF16)
    ps_g = p.ps("ps_g", [128, 512])
    ps_h = p.ps("ps_h", [128, 512])
    ps_r0 = p.ps("ps_r0", [1, 512])
    ps_r1 = p.ps("ps_r1", [1, 512])
    onesf = p.sb("onesf_m", [128, 128], F32)
    ln3 = LNAcc(p, "ln3", T, onesf, ps_r0, ps_r1, ps_g[:, 0:512])

    for c in range(NCH):
        p.dma("qs", xn[:, c * T:(c + 1) * T], xn2_d[c * 128:(c + 1) * 128, :])
    p.dma("qg", identb[:, :], id_d)
    p.dma("qg", onesb[:, :], ones_d)
    p.dma("qs", onesf[:, :], ones_d)
    load_weight_fm(p, wg1x, wg1x_d, D)
    p.dma("qg", wg1s[:, :], wg1s_d)
    load_weight_fm(p, wg2, wg2_d, 4)
    load_weight_fm(p, wres, wres_d, 4)
    p.dma("qg", wog1s[:, :], wog1s_d)
    p.dma("qg", bg1c[:, :], bg1_d.rearrange("(c p) one -> p (c one)", p=128), allow_slow_non_contiguous=True)
    p.dma("qg", bog1c[:, :], bog1_d.rearrange("(c p) one -> p (c one)", p=128), allow_slow_non_contiguous=True)
    p.dma("qg", bg2r[:, :], bg2_d)
    p.dma("qg", bresr[:, :], bres_d)
    p.dma("qg", bog2r[:, :], bog2_d)
    p.dma("qs", ltr[0:1, 0:1], lt_d)
    p.dma("qg", nwr[:, :], nw4_d)
    wog2c = p.sb("wog2c", [128, 4], BF16)
    p.dma("qg", wog2c[:, :], wog2_d.rearrange("(c p) one -> p (c one)", p=128), allow_slow_non_contiguous=True)

    p.act(ltr[0:1, 1:2], ltr[0:1, 0:1], AF.Abs)
    p.act(ltr[0:1, 2:3], ltr[0:1, 1:2], AF.Exp, scale=-1.0)
    p.ts(ltr[0:1, 2:3], ltr[0:1, 2:3], 1.0, None, ALU.add)
    p.act(ltr[0:1, 2:3], ltr[0:1, 2:3], AF.Ln)
    p.act(ltr[0:1, 3:4], ltr[0:1, 0:1], AF.Relu)
    p.tt(ltr[0:1, 3:4], ltr[0:1, 3:4], ltr[0:1, 2:3], ALU.add)
    p.recip(ltr[0:1, 3:4], ltr[0:1, 3:4])
    rspb = p.sb("rspb", [1, 1], BF16)
    p.copy(rspb[:, :], ltr[0:1, 3:4])
    p.mm(ps_g[:, 0:1], onesb[0:1, :], rspb[:, :])
    p.copy(spcol[:, :], ps_g[:, 0:1])
    crlr = p.sb("crlr", [1, 4], F32)
    crlb = p.sb("crlb", [1, 4], BF16)
    p.dma("qs", crlr[:, :], crl_d)
    p.act(crlr[:, :], crlr[:, :], AF.Sigmoid)
    p.copy(crlb[:, :], crlr[:, :])
    p.mm(ps_g[:, 0:4], onesb[0:1, :], crlb[:, :])
    p.copy(staticb[:, :], ps_g[:, 0:4])
    p.mm(ps_g[:, 0:512], onesb[0:1, :], nwr[:, 0:512])
    p.copy(nwb[:, 0:512], ps_g[:, 0:512])
    p.mm(ps_g[:, 0:512], onesb[0:1, :], nwr[:, 512:1024])
    p.copy(nwb[:, 512:1024], ps_g[:, 0:512])

    def head_stats(src_ap, dst, col0, extra_sq_f32=False):
        x3 = src_ap.rearrange("p (h v) -> p h v", v=256)
        p.tt(sqb[:, :], src_ap, src_ap, ALU.mult)
        sq3 = sqb[:, :].rearrange("p (h v) -> p h v", v=256)
        p.red(s4a[:, :], x3, ALU.add)
        p.red(s4b[:, :], sq3, ALU.add)
        p.red(s4c[:, :], x3, ALU.add, absval=True)
        p.ts(s4a[:, :], s4a[:, :], 1.0 / 256, None, ALU.mult)
        p.ts(s4c[:, :], s4c[:, :], 1.0 / 256, None, ALU.mult)
        p.tt(s4d[:, :], s4a[:, :], s4a[:, :], ALU.mult)
        p.ts(s4d[:, :], s4d[:, :], -1.0, None, ALU.mult)
        p.stt(s4d[:, :], s4b[:, :], 1.0 / 256, s4d[:, :], ALU.mult, ALU.add)
        p.act(s4b[:, :], s4b[:, :], AF.Sqrt)
        for k, buf in enumerate((s4a, s4d, s4c, s4b)):
            p.red(dst[:, col0 + k:col0 + k + 1], buf[:, :], ALU.add)
            p.ts(dst[:, col0 + k:col0 + k + 1], dst[:, col0 + k:col0 + k + 1],
                 0.25, None, ALU.mult)

    for ti in range(16):
        t0 = ti * 128
        p.dma("qs", tdel[:, :], del_d[t0:t0 + 128, :])
        p.dma("qs", tls[:, :], ls_d[:, t0:t0 + 128], transpose=True)
        p.dma("qs", tll[:, :], ll_d[:, t0:t0 + 128], transpose=True)
        p.dma("qs", tvd[:, :], vd_d[:, t0:t0 + 128], transpose=True)
        if dbg and ti < 4:
            p.dma("qg", dbg_d[:, ti * D:(ti + 1) * D], tls[:, :])
        head_stats(tls[:, :], st16, 0)
        head_stats(tll[:, :], st16, 4)
        head_stats(tdel[:, :], st16, 8)
        head_stats(tvd[:, :], st16, 12)
        vd_stats = (st16, 12)
        p.copy(st16b[:, :], st16[:, :])
        p.transp(ps_t[0:16, 0:128], st16b[:, :], identb[:, :])
        p.copy(stfm[:, :], ps_t[0:16, 0:128])
        for oc in range(NCH):
            for kc in range(NCH):
                p.mm(ps_g[:, 0:128], wg1x[:, kc * D + oc * 128: kc * D + oc * 128 + 128],
                     xn[:, kc * T + t0: kc * T + t0 + 128],
                     start=(kc == 0), stop=False)
            p.mm(ps_g[:, 0:128], wg1s[:, oc * 128:(oc + 1) * 128], stfm[:, :],
                 start=False, stop=True)
            p.act(gg[:, oc * 128:(oc + 1) * 128], ps_g[:, 0:128], AF.Gelu,
                  bias=bg1c[:, oc:oc + 1])
        for kc in range(NCH):
            p.mm(ps_h[:, 0:4], gg[:, kc * 128:(kc + 1) * 128], wg2[:, kc * 4:(kc + 1) * 4],
                 start=(kc == 0), stop=False)
        p.mm(ps_h[:, 0:4], onesb[0:1, :], bg2r[:, :], start=False, stop=True)
        p.copy(pvec[:, :], ps_h[:, 0:4])
        p.ts(pvec[:, :], pvec[:, :], spcol[:, 0:1], None, ALU.mult)
        p.red(s4a[:, 0:1], pvec[:, :], ALU.max)
        p.ts(s4a[:, 0:1], s4a[:, 0:1], -1.0, None, ALU.mult)
        p.act(pvec[:, :], pvec[:, :], AF.Exp, bias=s4a[:, 0:1])
        p.red(s4a[:, 1:2], pvec[:, :], ALU.add)
        p.recip(s4a[:, 1:2], s4a[:, 1:2])
        p.ts(pvec[:, :], pvec[:, :], s4a[:, 1:2], None, ALU.mult)
        p.ts(pvec[:, :], pvec[:, :], 0.02, None, ALU.max)
        p.red(s4a[:, 2:3], pvec[:, :], ALU.add)
        p.recip(s4a[:, 2:3], s4a[:, 2:3])
        p.ts(pvec[:, :], pvec[:, :], s4a[:, 2:3], None, ALU.mult)
        for kc in range(NCH):
            p.mm(ps_h[:, 0:4], xn[:, kc * T + t0: kc * T + t0 + 128],
                 wres[:, kc * 4:(kc + 1) * 4], start=(kc == 0), stop=False)
        p.mm(ps_h[:, 0:4], onesb[0:1, :], bresr[:, :], start=False, stop=True)
        p.act(rg[:, :], ps_h[:, 0:4], AF.Sigmoid)
        p.tt(rg[:, :], rg[:, :], staticb[:, :], ALU.mult)
        for h in range(4):
            slh = slice(h * 256, (h + 1) * 256)
            p.ts(obuf[:, slh], tls[:, slh], pvec[:, 0:1], None, ALU.mult)
            p.stt(obuf[:, slh], tll[:, slh], pvec[:, 1:2], obuf[:, slh], ALU.mult, ALU.add)
            p.stt(obuf[:, slh], tdel[:, slh], pvec[:, 2:3], obuf[:, slh], ALU.mult, ALU.add)
            p.stt(obuf[:, slh], tvd[:, slh], pvec[:, 3:4], obuf[:, slh], ALU.mult, ALU.add)
            p.stt(obuf[:, slh], tls[:, slh], rg[:, h:h + 1], obuf[:, slh], ALU.mult, ALU.add)
        head_stats(obuf[:, :], og8, 0)
        p.copy(og8[:, 4:8], st16[:, 12:16])
        p.copy(og8b[:, :], og8[:, :])
        p.transp(ps_t[0:8, 0:128], og8b[:, :], identb[:, :])
        p.copy(ogfm[:, :], ps_t[0:8, 0:128])
        for oc in range(4):
            p.dma("qg", wbuf[:, :].rearrange("p (c n) -> p c n", n=128),
                  wog1x_d[:, oc * 128:(oc + 1) * 128].rearrange("(c p) n -> p c n", p=128))
            for kc in range(NCH):
                p.mm(ps_g[:, 0:128], wbuf[:, kc * 128:(kc + 1) * 128],
                     xn[:, kc * T + t0: kc * T + t0 + 128],
                     start=(kc == 0), stop=False)
            p.mm(ps_g[:, 0:128], wog1s[:, oc * 128:(oc + 1) * 128], ogfm[:, :],
                 start=False, stop=True)
            p.act(ogb[:, oc * 128:(oc + 1) * 128], ps_g[:, 0:128], AF.Gelu,
                  bias=bog1c[:, oc:oc + 1])
        for kc in range(4):
            p.mm(ps_h[:, 0:1], ogb[:, kc * 128:(kc + 1) * 128], wog2c[:, kc:kc + 1],
                 start=(kc == 0), stop=False)
        p.mm(ps_h[:, 0:1], onesb[0:1, :], bog2r[:, :], start=False, stop=True)
        p.act(ogc[:, :], ps_h[:, 0:1], AF.Sigmoid)
        p.ts(obuf[:, :], obuf[:, :], ogc[:, 0:1], None, ALU.mult)
        p.tt(sqb[:, :], obuf[:, :], obuf[:, :], ALU.mult)
        p.red(rmsc[:, :], sqb[:, :].rearrange("p (h v) -> p h v", v=256), ALU.add)
        p.ts(rmsc[:, :], rmsc[:, :], 1.0 / 256, 1e-5, ALU.mult, ALU.add)
        p.act(rmsc[:, :], rmsc[:, :], AF.Sqrt)
        p.recip(rmsc[:, :], rmsc[:, :])
        for h in range(4):
            slh = slice(h * 256, (h + 1) * 256)
            p.stt(ob16[:, slh], obuf[:, slh], rmsc[:, h:h + 1], nwb[:, slh],
                  ALU.mult, ALU.mult)
        for c in range(NCH):
            p.transp(ps_t[:, :], ob16[:, c * 128:(c + 1) * 128], identb[:, :])
            p.copy(OFM[:, c * T + t0: c * T + t0 + 128], ps_t[:, :])

    for oc in range(NCH):
        p.dma("qg", wbuf[:, :].rearrange("p (c n) -> p c n", n=128),
              wo_d[:, oc * 128:(oc + 1) * 128].rearrange("(c p) n -> p c n", p=128))
        for t0 in range(0, T, 512):
            for kc in range(NCH):
                p.mm(ps_g[:, :], wbuf[:, kc * 128:(kc + 1) * 128],
                     OFM[:, kc * T + t0: kc * T + t0 + 512],
                     start=(kc == 0), stop=(kc == NCH - 1))
            p.dma("qs", x2t[:, :], x2_d[oc * 128:(oc + 1) * 128, t0:t0 + 512])
            p.tt(x3s[:, :], x2t[:, :], ps_g[:, :], ALU.add)
            p.dma("qs", x3_d[oc * 128:(oc + 1) * 128, t0:t0 + 512], x3s[:, :])
            ln3.acc(x3s[:, :], t0)

    ln3.apply(x3_d, xn3, x2t)
    for c in range(NCH):
        p.dma("qs", xn3_d[c * 128:(c + 1) * 128, :], xn3[:, c * T:(c + 1) * T])
    ph.end()
    return ph



def build_ffn1qk(P=None, ext=None):
    ph = Phase(P, "fq", ext)
    nc = ph.nc
    T = TPC
    xn_d = ph.dio("xn3", [D, T], BF16, "ExternalInput")
    wqT_d = ph.din("wqT", [D, D], F32)
    wkT_d = ph.din("wkT", [D, D], F32)
    winvT_d = ph.din("winvT", [D, D], F32)
    Q_d = ph.dio("Q", [D, T], BF16, "ExternalOutput")
    K_d = ph.dio("K", [D, T], BF16, "ExternalOutput")
    iQ_d = ph.dio("iQ", [D, T], BF16, "ExternalOutput")
    iK_d = ph.dio("iK", [D, T], BF16, "ExternalOutput")

    p = ph.begin()
    xn = p.sb("xn_s", [128, NCH * T], BF16)
    Qf = p.sb("Qf", [128, NCH * T], BF16)
    Kf = p.sb("Kf", [128, NCH * T], BF16)
    wbuf = p.sb("wbuf", [128, NCH * 128], BF16)
    ps_a = p.ps("ps_a", [128, 512])

    def proj_fm(p_, dst, src, w_d_):
        for oc in range(NCH):
            p_.dma("qg", wbuf[:, :].rearrange("p (c n) -> p c n", n=128),
                   w_d_[:, oc * 128:(oc + 1) * 128].rearrange("(c p) n -> p c n", p=128))
            for t0 in range(0, T, 512):
                for kc in range(NCH):
                    p_.mm(ps_a[:, :], wbuf[:, kc * 128:(kc + 1) * 128],
                          src[:, kc * T + t0: kc * T + t0 + 512],
                          start=(kc == 0), stop=(kc == NCH - 1))
                p_.copy(dst[:, oc * T + t0: oc * T + t0 + 512], ps_a[:, :])

    for c in range(NCH):
        p.dma("qs", xn[:, c * T:(c + 1) * T], xn_d[c * 128:(c + 1) * 128, :])
    proj_fm(p, Qf, xn, wqT_d)
    proj_fm(p, Kf, xn, wkT_d)
    for c in range(NCH):
        p.dma("qs", Q_d[c * 128:(c + 1) * 128, :], Qf[:, c * T:(c + 1) * T])
        p.dma("qs", K_d[c * 128:(c + 1) * 128, :], Kf[:, c * T:(c + 1) * T])
    proj_fm(p, xn, Qf, winvT_d)
    for c in range(NCH):
        p.dma("qs", iQ_d[c * 128:(c + 1) * 128, :], xn[:, c * T:(c + 1) * T])
    proj_fm(p, xn, Kf, winvT_d)
    for c in range(NCH):
        p.dma("qs", iK_d[c * 128:(c + 1) * 128, :], xn[:, c * T:(c + 1) * T])
    ph.end()
    return ph


def build_ffn1branch(P=None, name="fb", ext=None):
    ph = Phase(P, name, ext)
    nc = ph.nc
    T = TPC
    qs_d = ph.dio("qside", [D, T], BF16, "ExternalInput")
    ks_d = ph.dio("kside", [D, T], BF16, "ExternalInput")
    v_d = ph.din("vmat", [D, 512], F32)
    w_d = ph.din("wmat", [D, 512], F32)
    rw1T_d = ph.din("rw1T", [D, D], F32)
    rb1_d = ph.din("rb1", [D, 1], F32)
    rw2T_d = ph.din("rw2T", [D, 8], F32)
    rb2_d = ph.din("rb2", [1, 8], F32)
    eb_d = ph.din("eb", [1, 8], F32)
    ones_d = ph.din("ones", [128, 128], F32)
    xV_d = ph.dio("xV", [512, T], BF16, "ExternalOutput")
    cum_d = ph.dio("cum", [512, T], BF16, "ExternalOutput")
    tot_d = ph.dio("tot", [512, 1], F32, "ExternalOutput")
    wts_d = ph.dio("wts", [T, 8], F32, "ExternalOutput")

    p = ph.begin()
    qsb = p.sb("qsb", [128, NCH * T], BF16)
    ksb = p.sb("ksb", [128, NCH * T], BF16)
    hb = p.sb("hb", [128, NCH * T], BF16)
    cumf = p.sb("cumf", [128, 4 * T], F32)
    xVb = p.sb("xVb", [128, 4 * T], BF16)
    cumb = p.sb("cumb", [128, 4 * T], BF16)
    wbuf = p.sb("wbuf", [128, NCH * 128], BF16)
    rw2sb = p.sb("rw2sb", [128, NCH * 8], BF16)
    onesb = p.sb("onesb", [128, 128], BF16)
    rb1c = p.sb("rb1c", [128, 8], F32)
    rb2r = p.sb("rb2r", [1, 8], BF16)
    ebr = p.sb("ebr", [1, 8], BF16)
    w8 = p.sb("w8", [128, 8], F32)
    s8 = p.sb("s8", [128, 2], F32)
    tot = p.sb("tot_s", [128, 4], F32)
    ps_a = p.ps("ps_a", [128, 512])
    ps_b = p.ps("ps_b", [128, 512])

    for c in range(NCH):
        p.dma("qs", qsb[:, c * T:(c + 1) * T], qs_d[c * 128:(c + 1) * 128, :])
        p.dma("qs", ksb[:, c * T:(c + 1) * T], ks_d[c * 128:(c + 1) * 128, :])
    p.dma("qg", onesb[:, :], ones_d)
    p.dma("qg", rb1c[:, :], rb1_d.rearrange("(c p) one -> p (c one)", p=128), allow_slow_non_contiguous=True)
    p.dma("qg", rb2r[:, :], rb2_d)
    p.dma("qg", ebr[:, :], eb_d)
    load_weight_fm(p, rw2sb, rw2T_d, 8)

    for oc in range(4):
        p.dma("qg", wbuf[:, :].rearrange("p (c n) -> p c n", n=128),
              v_d[:, oc * 128:(oc + 1) * 128].rearrange("(c p) n -> p c n", p=128))
        for t0 in range(0, T, 512):
            for kc in range(NCH):
                p.mm(ps_a[:, :], wbuf[:, kc * 128:(kc + 1) * 128],
                     qsb[:, kc * T + t0: kc * T + t0 + 512],
                     start=(kc == 0), stop=(kc == NCH - 1))
            p.copy(xVb[:, oc * T + t0: oc * T + t0 + 512], ps_a[:, :])
        p.dma("qg", wbuf[:, :].rearrange("p (c n) -> p c n", n=128),
              w_d[:, oc * 128:(oc + 1) * 128].rearrange("(c p) n -> p c n", p=128))
        for t0 in range(0, T, 512):
            for kc in range(NCH):
                p.mm(ps_b[:, :], wbuf[:, kc * 128:(kc + 1) * 128],
                     ksb[:, kc * T + t0: kc * T + t0 + 512],
                     start=(kc == 0), stop=(kc == NCH - 1))
            p.copy(cumb[:, oc * T + t0: oc * T + t0 + 512], ps_b[:, :])
        p.scan_tt(cumf[:, oc * T:(oc + 1) * T], cumb[:, oc * T:(oc + 1) * T],
                  cumb[:, oc * T:(oc + 1) * T], 0.0, ALU.add, ALU.bypass)
        p.copy(tot[:, oc:oc + 1], cumf[:, (oc + 1) * T - 1:(oc + 1) * T])
        p.copy(cumb[:, oc * T:(oc + 1) * T], cumf[:, oc * T:(oc + 1) * T])
    for c in range(4):
        p.dma("qs", xV_d[c * 128:(c + 1) * 128, :], xVb[:, c * T:(c + 1) * T])
        p.dma("qs", cum_d[c * 128:(c + 1) * 128, :], cumb[:, c * T:(c + 1) * T])
    p.dma("qs", tot_d.rearrange("(c p) one -> p (c one)", p=128), tot[:, :],
          allow_slow_non_contiguous=True)

    for oc in range(NCH):
        p.dma("qg", wbuf[:, :].rearrange("p (c n) -> p c n", n=128),
              rw1T_d[:, oc * 128:(oc + 1) * 128].rearrange("(c p) n -> p c n", p=128))
        for t0 in range(0, T, 512):
            for kc in range(NCH):
                p.mm(ps_a[:, :], wbuf[:, kc * 128:(kc + 1) * 128],
                     qsb[:, kc * T + t0: kc * T + t0 + 512],
                     start=(kc == 0), stop=(kc == NCH - 1))
            p.act(hb[:, oc * T + t0: oc * T + t0 + 512], ps_a[:, :], AF.Gelu,
                  bias=rb1c[:, oc:oc + 1])
    for ti in range(T // 128):
        t0 = ti * 128
        for kc in range(NCH):
            p.mm(ps_b[:, 0:8], hb[:, kc * T + t0: kc * T + t0 + 128],
                 rw2sb[:, kc * 8:(kc + 1) * 8], start=(kc == 0), stop=False)
        p.mm(ps_b[:, 0:8], onesb[0:1, :], rb2r[:, :], start=False, stop=False)
        p.mm(ps_b[:, 0:8], onesb[0:1, :], ebr[:, :], start=False, stop=True)
        p.copy(w8[:, :], ps_b[:, 0:8])
        p.red(s8[:, 0:1], w8[:, :], ALU.max)
        p.ts(s8[:, 0:1], s8[:, 0:1], -1.0, None, ALU.mult)
        p.act(w8[:, :], w8[:, :], AF.Exp, bias=s8[:, 0:1])
        p.red(s8[:, 1:2], w8[:, :], ALU.add)
        p.recip(s8[:, 1:2], s8[:, 1:2])
        p.ts(w8[:, :], w8[:, :], s8[:, 1:2], None, ALU.mult)
        p.dma("qs", wts_d[t0:t0 + 128, :], w8[:, :])
    ph.end()
    return ph


def build_ffn1b(P=None, ext=None):
    ph = Phase(P, "fc", ext)
    nc = ph.nc
    T = TPC
    DIO = {"xV_f", "cum_f", "xV_i", "cum_i", "wts_f", "wts_i", "totf", "toti", "x3"}
    ins = {}
    for nm, shp, dt in (("xV_f", [512, T], BF16), ("cum_f", [512, T], BF16),
                        ("xV_i", [512, T], BF16), ("cum_i", [512, T], BF16),
                        ("wts_f", [T, 8], F32), ("wts_i", [T, 8], F32),
                        ("totf", [8, 512], F32), ("toti", [8, 512], F32),
                        ("maskr", [8, 1], F32), ("Uf", [512, D], F32),
                        ("bfm", [8, D], F32), ("Ui", [512, D], F32),
                        ("bim", [8, D], F32), ("woT", [D, D], F32),
                        ("alpha", [1, 1], F32), ("x3", [D, T], F32),
                        ("E8", [8, 512], F32), ("ones", [128, 128], F32),
                        ("ident", [128, 128], F32), ("BwT", [D, 64], F32),
                        ("Bb", [64, 1], F32), ("WihT", [D, 384], F32),
                        ("bih", [384, 1], F32), ("bhh_rz", [256, 1], F32),
                        ("toff", [1, 1], F32)):
        ins[nm] = (ph.dio(nm, shp, dt, "ExternalInput") if nm in DIO
                   else ph.din(nm, shp, dt))
    x4_d = ph.dio("x4", [D, T], F32, "ExternalOutput")
    xn4_d = ph.dio("xn4", [D, T], BF16, "ExternalOutput")
    bx_d = ph.dio("bx", [64, T], F32, "ExternalOutput")
    ih_d = ph.dio("ih", [384, T], F32, "ExternalOutput")

    p = ph.begin()
    xVb = p.sb("xVb", [128, 4 * T], BF16)
    cumb = p.sb("cumb", [128, 4 * T], BF16)
    prodb = p.sb("prodb", [128, 4 * T], BF16)
    fwdpre = p.sb("fwdpre", [128, NCH * T], BF16)
    xn4 = p.sb("xn4_s", [128, NCH * T], BF16)
    wbuf = p.sb("wbuf", [128, NCH * 128], BF16)
    onesb = p.sb("onesb", [128, 128], BF16)
    identb = p.sb("identb", [128, 128], BF16)
    E8b = p.sb("E8b", [8, 512], BF16)
    totb = p.sb("totb", [8, 512], BF16)
    maskb = p.sb("maskb", [8, 1], BF16)
    pref = p.sb("pref", [128, 4], F32)
    wfm_f = p.sb("wfm_f", [8, T], BF16)
    wfm_i = p.sb("wfm_i", [8, T], BF16)
    w8b = p.sb("w8b", [128, 8], BF16)
    bfsb = p.sb("bfsb", [8, D], BF16)
    bisb = p.sb("bisb", [8, D], BF16)
    invn = p.sb("invn", [1, T], F32)
    invnb = p.sb("invnb", [1, T], BF16)
    acol = p.sb("acol", [128, 1], F32)
    alr = p.sb("alr", [1, 1], BF16)
    tmpf = p.sb("tmpf", [128, 512], F32)
    x3t = p.sb("x3t", [128, 512], F32)
    x4s = p.sb("x4s", [128, 512], F32)
    bxs = p.sb("bxs", [64, T], F32)
    ihs = p.sb("ihs", [128, T], F32)
    onesf = p.sb("onesf_f", [128, 128], F32)
    bbc = p.sb("bbc", [64, 1], F32)
    bihc = p.sb("bihc", [128, 3], F32)
    bhhc = p.sb("bhhc", [128, 2], F32)

    ps_a = p.ps("ps_a", [128, 512])
    ps_b = p.ps("ps_b", [128, 512])
    ps_c = p.ps("ps_c", [128, 512])
    ps_t = p.ps("ps_t", [128, 128], BF16)
    ps_r0 = p.ps("ps_r0", [1, 512])
    ps_r1 = p.ps("ps_r1", [1, 512])
    ln4 = LNAcc(p, "ln4", T, onesf, ps_r0, ps_r1, ps_c[:, 0:512])

    p.dma("qg", onesb[:, :], ins["ones"])
    p.dma("qs", onesf[:, :], ins["ones"])
    p.dma("qg", identb[:, :], ins["ident"])
    p.dma("qg", E8b[:, :], ins["E8"])
    p.dma("qg", maskb[:, :], ins["maskr"])
    p.dma("qg", bfsb[:, :], ins["bfm"])
    p.dma("qg", bisb[:, :], ins["bim"])
    p.dma("qg", alr[:, :], ins["alpha"])
    p.dma("qg", bbc[:, :], ins["Bb"])
    p.dma("qg", bihc[:, :], ins["bih"].rearrange("(c p) one -> p (c one)", p=128),
          allow_slow_non_contiguous=True)
    p.dma("qg", bhhc[:, :], ins["bhh_rz"].rearrange("(c p) one -> p (c one)", p=128),
          allow_slow_non_contiguous=True)
    p.mm(ps_a[:, 0:1], onesb[0:1, :], alr[:, :])
    p.copy(acol[:, :], ps_a[:, 0:1])
    p.op("pool", lambda e: e.iota(invn[0:1, :], [[1, T]], base=0, channel_multiplier=0, allow_small_or_imprecise_dtypes=True), reads=(), writes=_tags(invn))
    tofc = p.sb("tofc", [1, 1], F32)
    p.dma("qs", tofc[:, :], ins["toff"])
    p.ts(invn[0:1, :], invn[0:1, :], tofc[0:1, 0:1], None, ALU.add)
    p.ts(invn[0:1, :], invn[0:1, :], 1.0, None, ALU.add)
    p.recip(invn[0:1, :], invn[0:1, :])
    p.copy(invnb[:, :], invn[0:1, :])

    def do_branch(xV_d_, cum_d_, wts_d_, tot_d_, U_d_, bsb, wfm, first):
        for ti in range(T // 128):
            p.dma("qg", w8b[:, :], wts_d_[ti * 128:(ti + 1) * 128, :])
            p.transp(ps_t[0:8, 0:128], w8b[:, :], identb[:, :])
            p.copy(wfm[:, ti * 128:(ti + 1) * 128], ps_t[0:8, 0:128])
        p.dma("qg", totb[:, :], tot_d_)
        for c in range(4):
            p.mm(ps_a[:, 0:1], totb[:, c * 128:(c + 1) * 128], maskb[:, :])
            p.copy(pref[:, c:c + 1], ps_a[:, 0:1])
        for c in range(4):
            p.dma("qs", xVb[:, c * T:(c + 1) * T], xV_d_[c * 128:(c + 1) * 128, :])
            p.dma("qs", cumb[:, c * T:(c + 1) * T], cum_d_[c * 128:(c + 1) * 128, :])
        for c in range(4):
            p.ts(cumb[:, c * T:(c + 1) * T], cumb[:, c * T:(c + 1) * T],
                 pref[:, c:c + 1], None, ALU.add)
            for t0 in range(0, T, 512):
                sl_ = slice(c * T + t0, c * T + t0 + 512)
                p.mm(ps_b[:, :], E8b[:, c * 128:(c + 1) * 128], wfm[:, t0:t0 + 512])
                p.tt(prodb[:, sl_], xVb[:, sl_], cumb[:, sl_], ALU.mult)
                p.tt(prodb[:, sl_], prodb[:, sl_], ps_b[:, :], ALU.mult)
        for oc in range(NCH):
            p.dma("qg", wbuf[:, 0:4 * 128].rearrange("p (c n) -> p c n", n=128),
                  U_d_[:, oc * 128:(oc + 1) * 128].rearrange("(c p) n -> p c n", p=128))
            for t0 in range(0, T, 512):
                for kc in range(4):
                    p.mm(ps_a[:, :], wbuf[:, kc * 128:(kc + 1) * 128],
                         prodb[:, kc * T + t0: kc * T + t0 + 512],
                         start=(kc == 0), stop=False)
                p.mm(ps_a[:, :], bsb[:, oc * 128:(oc + 1) * 128], wfm[:, t0:t0 + 512],
                     start=False, stop=True)
                sl_ = slice(oc * T + t0, oc * T + t0 + 512)
                if first:
                    p.copy(fwdpre[:, sl_], ps_a[:, :])
                else:
                    p.copy(tmpf[:, :], ps_a[:, :])
                    p.ts(tmpf[:, :], tmpf[:, :], acol[:, 0:1], None, ALU.mult)
                    p.tt(tmpf[:, :], tmpf[:, :], fwdpre[:, sl_], ALU.add)
                    p.mm(ps_c[:, :], onesb[0:1, :], invnb[:, t0:t0 + 512])
                    p.tt(fwdpre[:, sl_], tmpf[:, :], ps_c[:, :], ALU.mult)

    do_branch(ins["xV_f"], ins["cum_f"], ins["wts_f"], ins["totf"], ins["Uf"], bfsb,
              wfm_f, True)
    do_branch(ins["xV_i"], ins["cum_i"], ins["wts_i"], ins["toti"], ins["Ui"], bisb,
              wfm_i, False)

    for oc in range(NCH):
        p.dma("qg", wbuf[:, :].rearrange("p (c n) -> p c n", n=128),
              ins["woT"][:, oc * 128:(oc + 1) * 128].rearrange("(c p) n -> p c n", p=128))
        for t0 in range(0, T, 512):
            for kc in range(NCH):
                p.mm(ps_a[:, :], wbuf[:, kc * 128:(kc + 1) * 128],
                     fwdpre[:, kc * T + t0: kc * T + t0 + 512],
                     start=(kc == 0), stop=(kc == NCH - 1))
            p.dma("qs", x3t[:, :], ins["x3"][oc * 128:(oc + 1) * 128, t0:t0 + 512])
            p.tt(x4s[:, :], x3t[:, :], ps_a[:, :], ALU.add)
            p.dma("qs", x4_d[oc * 128:(oc + 1) * 128, t0:t0 + 512], x4s[:, :])
            ln4.acc(x4s[:, :], t0)

    ln4.apply(x4_d, xn4, x3t)
    for c in range(NCH):
        p.dma("qs", xn4_d[c * 128:(c + 1) * 128, :], xn4[:, c * T:(c + 1) * T])

    bwsb = prodb[:, 0:NCH * 64]
    load_weight_fm(p, bwsb, ins["BwT"], 64)
    for t0 in range(0, T, 512):
        for kc in range(NCH):
            p.mm(ps_a[0:64, :], bwsb[:, kc * 64:(kc + 1) * 64],
                 xn4[:, kc * T + t0: kc * T + t0 + 512],
                 start=(kc == 0), stop=(kc == NCH - 1))
        p.ts(bxs[0:64, t0:t0 + 512], ps_a[0:64, :], bbc[:, 0:1], None, ALU.add)
    p.dma("qs", bx_d, bxs[0:64, :])
    wihsb = prodb[:, NCH * 64:NCH * 64 + NCH * 384]
    load_weight_fm(p, wihsb, ins["WihT"], 384)
    for g in range(3):
        for t0 in range(0, T, 512):
            for kc in range(NCH):
                p.mm(ps_b[:, :], wihsb[:, kc * 384 + g * 128: kc * 384 + (g + 1) * 128],
                     xn4[:, kc * T + t0: kc * T + t0 + 512],
                     start=(kc == 0), stop=(kc == NCH - 1))
            if g < 2:
                p.ts(ihs[:, t0:t0 + 512], ps_b[:, :], bihc[:, g:g + 1],
                     bhhc[:, g:g + 1], ALU.add, ALU.add)
            else:
                p.ts(ihs[:, t0:t0 + 512], ps_b[:, :], bihc[:, g:g + 1], None, ALU.add)
        p.dma("qs", ih_d[g * 128:(g + 1) * 128, :], ihs[:, :])
    ph.end()
    return ph



def build_scan(SEG, P=None, ext=None):
    ph = Phase(P, "sc", ext)
    nc = ph.nc
    AT_d = ph.din("AT", [64, 64], F32)
    WhT_d = ph.din("WhT", [128, 384], F32)
    bhhn_d = ph.din("bhhn", [1, 128], F32)
    bx_d = ph.dio("bx", [64, SEG], F32, "ExternalInput")
    ih_d = ph.dio("ih", [384, SEG], F32, "ExternalInput")
    hs0_d = ph.din("hs0", [64, 1], F32)
    hr0_d = ph.din("hr0", [128, 1], F32)
    id_d = ph.din("ident", [128, 128], F32)
    hsh_d = ph.dio("hsh", [64, SEG], BF16, "ExternalOutput")
    hrh_d = ph.dio("hrh", [128, SEG], BF16, "ExternalOutput")
    hsf_d = ph.dio("hsf", [64, 1], F32, "ExternalOutput")
    hrf_d = ph.dio("hrf", [128, 1], F32, "ExternalOutput")
    p = ph.begin()
    emit_scan(p, SEG, dict(AT=AT_d, WhT=WhT_d, bhhn=bhhn_d, bx=bx_d, ih=ih_d,
                           hs0=hs0_d, hr0=hr0_d, ident=id_d, hsh=hsh_d, hrh=hrh_d,
                           hsf=hsf_d, hrf=hrf_d), "")
    ph.end()
    return ph


def emit_scan(p, SEG, io, pfx):
    ATb = p.sb(pfx + "ATb", [64, 64], BF16)
    Wrb = p.sb(pfx + "Wrb", [128, 128], BF16)
    Wzb = p.sb(pfx + "Wzb", [128, 128], BF16)
    Wnb = p.sb(pfx + "Wnb", [128, 128], BF16)
    idb = p.sb(pfx + "idb", [128, 128], BF16)
    ihr = p.sb(pfx + "ihr", [128, SEG], BF16)
    ihz = p.sb(pfx + "ihz", [128, SEG], BF16)
    ihn = p.sb(pfx + "ihn", [128, SEG], F32)
    bhnr = p.sb(pfx + "bhnr", [1, 128], BF16)
    onesr = p.sb(pfx + "onesr", [1, 128], BF16)
    sigs = p.sb(pfx + "sigs", [64, 1], F32)
    bxb = p.sb(pfx + "bxb", [64, SEG], BF16)
    hsB = p.sb(pfx + "hsB", [64, SEG + 1], BF16)
    hrB = p.sb(pfx + "hrB", [128, SEG + 1], BF16)
    rz = p.sb(pfx + "rz", [128, 2], F32)
    mcol = p.sb(pfx + "mcol", [128, 1], F32)
    ncol = p.sb(pfx + "ncol", [128, 1], F32)
    dcol = p.sb(pfx + "dcol", [128, 1], F32)
    hsf32 = p.sb(pfx + "hsf32", [64, 1], F32)
    hrf32 = p.sb(pfx + "hrf32", [128, 1], F32)
    ps_g0 = p.ps(pfx + "ps_g0", [128, 512])
    ps_g1 = p.ps(pfx + "ps_g1", [128, 512])
    ps_s0 = p.ps(pfx + "ps_s0", [64, 512])
    ps_s1 = p.ps(pfx + "ps_s1", [64, 512])

    p.dma("qg", ATb[:, :], io["AT"])
    p.dma("qg", Wrb[:, :], io["WhT"][:, 0:128])
    p.dma("qg", Wzb[:, :], io["WhT"][:, 128:256])
    p.dma("qg", Wnb[:, :], io["WhT"][:, 256:384])
    p.dma("qg", bhnr[:, :], io["bhhn"])
    p.memset(onesr[:, :], 1.0)
    p.dma("qg", ihr[:, :], io["ih"][0:128, :])
    p.dma("qg", ihz[:, :], io["ih"][128:256, :])
    p.dma("qs", ihn[:, :], io["ih"][256:384, :])
    p.dma("qg", bxb[:, :], io["bx"])
    p.dma("qg", hsB[:, 0:1], io["hs0"])
    p.dma("qg", hrB[:, 0:1], io["hr0"])
    p.dma("qg", idb[:, :], io["ident"])

    BLK = 128
    NB = SEG // BLK
    for blk in range(NB):
        t0 = blk * BLK
        ps_g = ps_g0 if blk % 2 == 0 else ps_g1
        ps_s = ps_s0 if blk % 2 == 0 else ps_s1
        p.mm(ps_g[:, 0:BLK], idb[:, :], ihr[:, t0:t0 + BLK], start=True, stop=True)
        p.mm(ps_g[:, BLK:2 * BLK], idb[:, :], ihz[:, t0:t0 + BLK], start=False, stop=True)
        p.mm(ps_g[:, 2 * BLK:3 * BLK], bhnr[:, :], onesr[:, :], start=False, stop=True)
        p.mm(ps_s[:, 0:BLK], idb[0:64, 0:64], bxb[:, t0:t0 + BLK], start=True, stop=True)
        g3 = ps_g[:, 0:2 * BLK].rearrange("p (b t) -> p b t", b=2)
        for i in range(BLK):
            t = t0 + i
            p.mm(ps_g[:, i:i + 1], Wrb[:, :], hrB[:, t:t + 1], start=False, stop=True)
            p.mm(ps_g[:, BLK + i:BLK + i + 1], Wzb[:, :], hrB[:, t:t + 1],
                 start=False, stop=True)
            p.mm(ps_g[:, 2 * BLK + i:2 * BLK + i + 1], Wnb[:, :], hrB[:, t:t + 1],
                 start=False, stop=True)
            p.mm(ps_s[:, i:i + 1], ATb[:, :], hsB[:, t:t + 1], start=False, stop=True)
            p.act(rz[:, :], g3[:, :, i], AF.Sigmoid)
            p.act(ncol[:, :], ps_g[:, 2 * BLK + i:2 * BLK + i + 1], AF.Tanh,
                  bias=ihn[:, t:t + 1], scale=rz[:, 0:1])
            p.act(sigs[:, :], ps_s[:, i:i + 1], AF.Sigmoid)
            p.tt(hsB[:, t + 1:t + 2], ps_s[:, i:i + 1], sigs[:, :], ALU.mult)
            p.tt(dcol[:, :], hrB[:, t:t + 1], ncol[:, :], ALU.subtract)
            p.stt(hrB[:, t + 1:t + 2], dcol[:, :], rz[:, 1:2], ncol[:, :],
                  ALU.mult, ALU.add)

    p.dma("qs", io["hsh"], hsB[:, 1:SEG + 1])
    p.dma("qs", io["hrh"], hrB[:, 1:SEG + 1])
    p.copy(hsf32[:, :], hsB[:, SEG:SEG + 1])
    p.copy(hrf32[:, :], hrB[:, SEG:SEG + 1])
    p.dma("qs", io["hsf"], hsf32[:, :])
    p.dma("qs", io["hrf"], hrf32[:, :])



def build_post2a(P=None, ext=None):
    ph = Phase(P, "pa", ext)
    nc = ph.nc
    T = TPC
    DIO = {"x4", "xn4", "hs", "hr"}
    ins = {}
    for nm, shp, dt in (("x4", [D, T], F32), ("xn4", [D, T], BF16),
                        ("hs", [64, T], BF16), ("hr", [128, T], BF16),
                        ("CwT", [64, D], F32), ("DwT", [D, D], F32),
                        ("cdb", [D, 1], F32),
                        ("cb", [D, 1], F32), ("db", [D, 1], F32),
                        ("projT", [128, D], F32), ("projb", [D, 1], F32),
                        ("gateT", [D, 2], F32), ("gateb", [2, 1], F32),
                        ("ones", [128, 128], F32), ("ident", [128, 128], F32)):
        ins[nm] = (ph.dio(nm, shp, dt, "ExternalInput") if nm in DIO
                   else ph.din(nm, shp, dt))
    x5_d = ph.dio("x5", [D, T], F32, "ExternalOutput")
    xn5_d = ph.dio("xn5", [D, T], BF16, "ExternalOutput")

    p = ph.begin()
    xn4 = p.sb("xn4_s", [128, NCH * T], BF16)
    xn5 = p.sb("xn5_s", [128, NCH * T], BF16)
    hs = p.sb("hs_s", [64, T], BF16)
    hr = p.sb("hr_s", [128, T], BF16)
    cwsb = p.sb("cwsb", [64, D], BF16)
    dwsb = p.sb("dwsb", [128, NCH * D], BF16)
    prsb = p.sb("prsb", [128, D], BF16)
    gtsb = p.sb("gtsb", [128, NCH * 2], BF16)
    cbc = p.sb("cbc", [128, 8], F32)
    dbc = p.sb("dbc", [128, 8], F32)
    pbc = p.sb("pbc", [128, 8], F32)
    gbc = p.sb("gbc", [2, 1], F32)
    gx = p.sb("gx", [2, T], F32)
    gxb = p.sb("gxb", [1, 2 * T], BF16)
    id2b = p.sb("id2b", [2, 2], BF16)
    ysd = p.sb("ysd", [128, 512], F32)
    yrn = p.sb("yrn", [128, 512], F32)
    fus = p.sb("fus", [128, 512], F32)
    x4t = p.sb("x4t", [128, 512], F32)
    onesb = p.sb("onesb", [128, 128], BF16)
    onesf = p.sb("onesf", [128, 128], F32)
    gx2b = p.sb("gx2b", [2, T], BF16)

    ps_a = p.ps("ps_a", [128, 512])
    ps_b = p.ps("ps_b", [128, 512])
    ps_c = p.ps("ps_c", [128, 512])
    ps_r0 = p.ps("ps_r0", [1, 512])
    ps_r1 = p.ps("ps_r1", [1, 512])
    ln5 = LNAcc(p, "ln5", T, onesf, ps_r0, ps_r1, ps_c[:, 0:512])

    for c in range(NCH):
        p.dma("qs", xn4[:, c * T:(c + 1) * T], ins["xn4"][c * 128:(c + 1) * 128, :])
    p.dma("qg", hs[:, :], ins["hs"])
    p.dma("qg", hr[:, :], ins["hr"])
    p.dma("qg", cwsb[:, :], ins["CwT"])
    load_weight_fm(p, dwsb, ins["DwT"], D)
    p.dma("qg", prsb[:, :], ins["projT"])
    load_weight_fm(p, gtsb, ins["gateT"], 2)
    p.dma("qg", cbc[:, :], ins["cb"].rearrange("(c p) one -> p (c one)", p=128), allow_slow_non_contiguous=True)
    p.dma("qg", dbc[:, :], ins["db"].rearrange("(c p) one -> p (c one)", p=128), allow_slow_non_contiguous=True)
    p.dma("qg", pbc[:, :], ins["projb"].rearrange("(c p) one -> p (c one)", p=128), allow_slow_non_contiguous=True)
    p.dma("qs", gbc[:, :], ins["gateb"])
    p.dma("qg", onesb[:, :], ins["ones"])
    p.dma("qs", onesf[:, :], ins["ones"])

    for t0 in range(0, T, 512):
        for kc in range(NCH):
            p.mm(ps_a[0:2, :], gtsb[:, kc * 2:(kc + 1) * 2],
                 xn4[:, kc * T + t0: kc * T + t0 + 512],
                 start=(kc == 0), stop=(kc == NCH - 1))
        p.ts(gx[:, t0:t0 + 512], ps_a[0:2, :], gbc[0:2, 0:1], None, ALU.add)
        p.act(gx[:, t0:t0 + 512], gx[:, t0:t0 + 512], AF.Sigmoid)
    p.dma("qg", id2b[:, :], ins["ident"][0:2, 0:2])
    p.copy(gx2b[:, :], gx[:, :])
    for g in range(2):
        for t0 in range(0, T, 512):
            p.mm(ps_a[0:1, :], id2b[:, g:g + 1], gx2b[:, t0:t0 + 512])
            p.copy(gxb[0:1, g * T + t0: g * T + t0 + 512], ps_a[0:1, :])

    for oc in range(NCH):
        for t0 in range(0, T, 512):
            p.mm(ps_a[:, :], cwsb[:, oc * 128:(oc + 1) * 128], hs[:, t0:t0 + 512],
                 start=True, stop=False)
            for kc in range(NCH):
                p.mm(ps_a[:, :], dwsb[:, kc * D + oc * 128: kc * D + oc * 128 + 128],
                     xn4[:, kc * T + t0: kc * T + t0 + 512],
                     start=False, stop=(kc == NCH - 1))
            p.ts(ysd[:, :], ps_a[:, :], cbc[:, oc:oc + 1], dbc[:, oc:oc + 1],
                 ALU.add, ALU.add)
            p.mm(ps_b[:, :], prsb[:, oc * 128:(oc + 1) * 128], hr[:, t0:t0 + 512])
            p.ts(yrn[:, :], ps_b[:, :], pbc[:, oc:oc + 1], None, ALU.add)
            p.mm(ps_c[:, :], onesb[0:1, :], gxb[0:1, t0:t0 + 512])
            p.tt(fus[:, :], ysd[:, :], ps_c[:, :], ALU.mult)
            p.mm(ps_c[:, :], onesb[0:1, :], gxb[0:1, T + t0: T + t0 + 512])
            p.tt(yrn[:, :], yrn[:, :], ps_c[:, :], ALU.mult)
            p.tt(fus[:, :], fus[:, :], yrn[:, :], ALU.add)
            p.dma("qs", x4t[:, :], ins["x4"][oc * 128:(oc + 1) * 128, t0:t0 + 512])
            p.tt(fus[:, :], fus[:, :], x4t[:, :], ALU.add)
            p.dma("qs", x5_d[oc * 128:(oc + 1) * 128, t0:t0 + 512], fus[:, :])
            ln5.acc(fus[:, :], t0)

    ln5.apply(x5_d, xn5, x4t)
    for c in range(NCH):
        p.dma("qs", xn5_d[c * 128:(c + 1) * 128, :], xn5[:, c * T:(c + 1) * T])
    ph.end()
    return ph



def build_ffn2(P=None, ext=None):
    ph = Phase(P, "f2", ext)
    nc = ph.nc
    T = TPC
    NE = 9
    HID = 2816
    NHC = HID // 128
    TH = T // 2
    ins = {}
    for nm, shp, dt in (("x5", [D, T], F32), ("xn5", [D, T], BF16),
                        ("g9T", [D, NE], F32), ("g9b", [1, NE], F32),
                        ("ew1T", [NE * D, HID], F32), ("ew3T", [NE * D, HID], F32),
                        ("ew2T", [NE * HID, D], F32),
                        ("ones", [128, 128], F32)):
        ins[nm] = (ph.dio(nm, shp, dt, "ExternalInput") if nm in ("x5", "xn5")
                   else ph.din(nm, shp, dt))
    x6_d = ph.dio("x6", [D, T], F32, "ExternalOutput")
    xn6_d = ph.dio("xn6", [D, T], BF16, "ExternalOutput")

    p = ph.begin()
    xn5 = p.sb("xn5_s", [128, NCH * T], BF16)
    acc = p.sb("acc", [128, 16 * D], BF16)
    hfm = p.sb("hfm", [128, NHC * TH], BF16)
    w2sb = p.sb("w2sb", [128, NHC * D], BF16)
    g9sb = p.sb("g9sb", [128, NCH * NE], BF16)
    g9br = p.sb("g9br", [1, NE], BF16)
    cw = p.sb("cw", [128, 16 * NE], F32)
    s9 = p.sb("s9", [128, NE], F32)
    e9 = p.sb("e9", [128, NE], F32)
    m1 = p.sb("m1", [128, 4], F32)
    wbufA = p.sb("wbufA", [128, NCH * 128], BF16)
    wbufB = p.sb("wbufB", [128, NCH * 128], BF16)
    onesb = p.sb("onesb", [128, 128], BF16)
    tmp = p.sb("tmp", [128, 512], BF16)
    x5t = p.sb("x5t", [128, 512], F32)
    x6s = p.sb("x6s", [128, 512], F32)
    tr6 = p.sb("tr6", [128, 512], BF16)
    onesf = p.sb("onesf", [128, 128], F32)

    ps_a = p.ps("ps_a", [128, 512])
    ps_b = p.ps("ps_b", [128, 512])
    ps_r0 = p.ps("ps_r0", [1, 512])
    ps_r1 = p.ps("ps_r1", [1, 512])

    for c in range(NCH):
        p.dma("qs", xn5[:, c * T:(c + 1) * T], ins["xn5"][c * 128:(c + 1) * 128, :])
    load_weight_fm(p, g9sb, ins["g9T"], NE)
    p.dma("qg", g9br[:, :], ins["g9b"])
    p.dma("qg", onesb[:, :], ins["ones"])
    p.dma("qs", onesf[:, :], ins["ones"])
    ln6 = LNAcc(p, "ln6", T, onesf, ps_r0, ps_r1, ps_a[:, 0:512])
    p.memset(acc[:, :], 0.0)

    for ti in range(16):
        t0 = ti * 128
        cwt = cw[:, ti * NE:(ti + 1) * NE]
        for kc in range(NCH):
            p.mm(ps_a[:, 0:NE], xn5[:, kc * T + t0: kc * T + t0 + 128],
                 g9sb[:, kc * NE:(kc + 1) * NE], start=(kc == 0), stop=False)
        p.mm(ps_a[:, 0:NE], onesb[0:1, :], g9br[:, :], start=False, stop=True)
        p.act(s9[:, :], ps_a[:, 0:NE], AF.Sigmoid)
        p.red(m1[:, 0:1], s9[:, :], ALU.max)
        p.ts(cwt, s9[:, :], m1[:, 0:1], None, ALU.is_equal)
        p.stt(e9[:, :], cwt, -1e30, s9[:, :], ALU.mult, ALU.add)
        p.red(m1[:, 1:2], e9[:, :], ALU.max)
        p.ts(e9[:, :], s9[:, :], m1[:, 1:2], None, ALU.is_equal)
        p.tt(m1[:, 2:3], m1[:, 0:1], m1[:, 1:2], ALU.add)
        p.ts(m1[:, 2:3], m1[:, 2:3], 1e-20, None, ALU.add)
        p.recip(m1[:, 2:3], m1[:, 2:3])
        p.ts(m1[:, 0:1], m1[:, 0:1], m1[:, 2:3], None, ALU.mult)
        p.ts(m1[:, 1:2], m1[:, 1:2], m1[:, 2:3], None, ALU.mult)
        p.ts(cwt, cwt, m1[:, 0:1], None, ALU.mult)
        p.stt(cwt, e9[:, :], m1[:, 1:2], cwt, ALU.mult, ALU.add)

    for e in range(NE):
        for kc in range(NHC):
            p.dma("qg", w2sb[:, kc * D:(kc + 1) * D],
                  ins["ew2T"][e * HID + kc * 128: e * HID + (kc + 1) * 128, :])
        for half in range(2):
            tb = half * TH
            for oc in range(NHC):
                p.dma("qg", wbufA[:, :].rearrange("p (c n) -> p c n", n=128),
                      ins["ew1T"][e * D:(e + 1) * D, oc * 128:(oc + 1) * 128]
                      .rearrange("(c p) n -> p c n", p=128))
                p.dma("qg", wbufB[:, :].rearrange("p (c n) -> p c n", n=128),
                      ins["ew3T"][e * D:(e + 1) * D, oc * 128:(oc + 1) * 128]
                      .rearrange("(c p) n -> p c n", p=128))
                for t0 in range(0, TH, 512):
                    for kc in range(NCH):
                        p.mm(ps_a[:, :], wbufA[:, kc * 128:(kc + 1) * 128],
                             xn5[:, kc * T + tb + t0: kc * T + tb + t0 + 512],
                             start=(kc == 0), stop=(kc == NCH - 1))
                        p.mm(ps_b[:, :], wbufB[:, kc * 128:(kc + 1) * 128],
                             xn5[:, kc * T + tb + t0: kc * T + tb + t0 + 512],
                             start=(kc == 0), stop=(kc == NCH - 1))
                    p.act(tmp[:, :], ps_a[:, :], AF.Silu)
                    p.tt(hfm[:, oc * TH + t0: oc * TH + t0 + 512], tmp[:, :],
                         ps_b[:, :], ALU.mult)
            for ti in range(TH // 128):
                tg = half * (TH // 128) + ti
                t0 = ti * 128
                for ds in range(2):
                    for kc in range(NHC):
                        p.mm(ps_a[:, :], hfm[:, kc * TH + t0: kc * TH + t0 + 128],
                             w2sb[:, kc * D + ds * 512: kc * D + ds * 512 + 512],
                             start=(kc == 0), stop=(kc == NHC - 1))
                    sl_ = slice(tg * D + ds * 512, tg * D + ds * 512 + 512)
                    p.stt(acc[:, sl_], ps_a[:, :], cw[:, tg * NE + e: tg * NE + e + 1],
                          acc[:, sl_], ALU.mult, ALU.add)

    identb = p.sb("identb2", [128, 128], BF16)
    id_d2 = ph.din("ident", [128, 128], F32)
    p.dma("qg", identb[:, :], id_d2)
    ps_t = p.ps("ps_t", [128, 128], BF16)
    for oc in range(NCH):
        for t0 in range(0, T, 512):
            for q4 in range(4):
                ti = t0 // 128 + q4
                p.transp(ps_t[:, :], acc[:, ti * D + oc * 128: ti * D + (oc + 1) * 128],
                         identb[:, :])
                p.copy(tr6[:, q4 * 128:(q4 + 1) * 128], ps_t[:, :])
            p.dma("qs", x5t[:, :], ins["x5"][oc * 128:(oc + 1) * 128, t0:t0 + 512])
            p.tt(x6s[:, :], x5t[:, :], tr6[:, :], ALU.add)
            p.dma("qs", x6_d[oc * 128:(oc + 1) * 128, t0:t0 + 512], x6s[:, :])
            ln6.acc(x6s[:, :], t0)
    xn6 = hfm[:, 0:NCH * T]
    ln6.apply(x6_d, xn6, x5t)
    for c in range(NCH):
        p.dma("qs", xn6_d[c * 128:(c + 1) * 128, :], xn6[:, c * T:(c + 1) * T])
    ph.end()
    return ph



def build_l3a(P=None, ext=None):
    ph = Phase(P, "la", ext)
    nc = ph.nc
    T = TPC
    DIO = {"x6", "xnA", "xnB"}
    ins = {}
    for nm, shp, dt in (("x6", [D, T], F32), ("xnA", [D, T], BF16),
                        ("xnB", [D, T], BF16), ("c1", [1, 1], F32),
                        ("c2", [1, 1], F32), ("c3", [1, 1], F32), ("c4", [1, 1], F32),
                        ("qpT", [D, D], F32), ("kpT", [D, D], F32), ("vpT", [D, D], F32),
                        ("qbr", [1, D], F32), ("kbr", [1, D], F32), ("vbr", [1, D], F32),
                        ("ones", [128, 128], F32)):
        ins[nm] = (ph.dio(nm, shp, dt, "ExternalInput") if nm in DIO
                   else ph.din(nm, shp, dt))
    x7_d = ph.dio("x7", [D, T], F32, "ExternalOutput")
    xn7_d = ph.dio("xn7", [D, T], BF16, "ExternalOutput")
    qt_d = ph.dio("qt", [T, D], BF16, "ExternalOutput")
    kt_d = ph.dio("kt", [T, D], BF16, "ExternalOutput")
    vt_d = ph.dio("vt", [T, D], BF16, "ExternalOutput")

    p = ph.begin()
    xnA = p.sb("xnA_s", [128, NCH * T], BF16)
    xnB = p.sb("xnB_s", [128, NCH * T], BF16)
    xn7 = p.sb("xn7_s", [128, NCH * T], BF16)
    sq = p.sb("sq", [128, 512], BF16)
    srows = p.sb("srows", [1, 3 * T], F32)
    arows = p.sb("arows", [1, 2 * T], F32)
    arb = p.sb("arb", [1, 2 * T], BF16)
    onesb = p.sb("onesb", [128, 128], BF16)
    ccol = p.sb("ccol", [1, 4], F32)
    x6t = p.sb("x6t", [128, 512], F32)
    x7s = p.sb("x7s", [128, 512], F32)
    wbuf = p.sb("wbuf", [128, NCH * 512], BF16)
    brow = p.sb("brow", [1, D], BF16)
    ttile = p.sb("ttile", [128, D], BF16)
    onesf = p.sb("onesf", [128, 128], F32)

    ps_a = p.ps("ps_a", [128, 512])
    ps_b = p.ps("ps_b", [128, 512])
    ps_r0 = p.ps("ps_r0", [1, 512])
    ps_r1 = p.ps("ps_r1", [1, 512])

    for c in range(NCH):
        p.dma("qs", xnA[:, c * T:(c + 1) * T], ins["xnA"][c * 128:(c + 1) * 128, :])
        p.dma("qs", xnB[:, c * T:(c + 1) * T], ins["xnB"][c * 128:(c + 1) * 128, :])
    p.dma("qg", onesb[:, :], ins["ones"])
    p.dma("qs", onesf[:, :], ins["ones"])
    ln7 = LNAcc(p, "ln7", T, onesf, ps_r0, ps_r1, ps_b[:, 0:512])
    p.dma("qs", ccol[0:1, 0:1], ins["c1"])
    p.dma("qs", ccol[0:1, 1:2], ins["c2"])
    p.dma("qs", ccol[0:1, 2:3], ins["c3"])
    p.dma("qs", ccol[0:1, 3:4], ins["c4"])

    for (idx, (ta, tb)) in enumerate(((xnA, xnA), (xnA, xnB), (xnB, xnB))):
        for t0 in range(0, T, 512):
            for c in range(NCH):
                p.tt(sq[:, :], ta[:, c * T + t0: c * T + t0 + 512],
                     tb[:, c * T + t0: c * T + t0 + 512], ALU.mult)
                p.mm(ps_a[0:1, :], onesb[:, 0:1], sq[:, :], start=(c == 0),
                     stop=(c == NCH - 1))
            p.act(srows[0:1, idx * T + t0: idx * T + t0 + 512], ps_a[0:1, :],
                  AF.Copy, scale=1.0 / 32.0)
    p.ts(arows[0:1, 0:T], srows[0:1, 0:T], ccol[0:1, 0:1], None, ALU.mult)
    p.stt(arows[0:1, 0:T], srows[0:1, T:2 * T], ccol[0:1, 1:2], arows[0:1, 0:T],
          ALU.mult, ALU.add)
    p.ts(arows[0:1, T:2 * T], srows[0:1, T:2 * T], ccol[0:1, 2:3], None, ALU.mult)
    p.stt(arows[0:1, T:2 * T], srows[0:1, 2 * T:3 * T], ccol[0:1, 3:4],
          arows[0:1, T:2 * T], ALU.mult, ALU.add)
    p.tt(srows[0:1, 0:T], arows[0:1, 0:T], arows[0:1, T:2 * T], ALU.max)
    p.tt(arows[0:1, 0:T], arows[0:1, 0:T], srows[0:1, 0:T], ALU.subtract)
    p.tt(arows[0:1, T:2 * T], arows[0:1, T:2 * T], srows[0:1, 0:T], ALU.subtract)
    p.act(arows[0:1, 0:T], arows[0:1, 0:T], AF.Exp)
    p.act(arows[0:1, T:2 * T], arows[0:1, T:2 * T], AF.Exp)
    p.tt(srows[0:1, 0:T], arows[0:1, 0:T], arows[0:1, T:2 * T], ALU.add)
    p.recip(srows[0:1, 0:T], srows[0:1, 0:T])
    p.tt(arows[0:1, 0:T], arows[0:1, 0:T], srows[0:1, 0:T], ALU.mult)
    p.tt(arows[0:1, T:2 * T], arows[0:1, T:2 * T], srows[0:1, 0:T], ALU.mult)
    p.copy(arb[:, :], arows[0:1, :])
    for oc in range(NCH):
        for t0 in range(0, T, 512):
            p.mm(ps_a[:, :], onesb[0:1, :], arb[0:1, t0:t0 + 512])
            p.mm(ps_b[:, :], onesb[0:1, :], arb[0:1, T + t0:T + t0 + 512])
            sl_ = slice(oc * T + t0, oc * T + t0 + 512)
            p.dma("qs", x6t[:, :], ins["x6"][oc * 128:(oc + 1) * 128, t0:t0 + 512])
            p.tt(x7s[:, :], xnA[:, sl_], ps_a[:, :], ALU.mult)
            p.tt(sq[:, :], xnB[:, sl_], ps_b[:, :], ALU.mult)
            p.tt(x7s[:, :], x7s[:, :], sq[:, :], ALU.add)
            p.tt(x7s[:, :], x7s[:, :], x6t[:, :], ALU.add)
            p.dma("qs", x7_d[oc * 128:(oc + 1) * 128, t0:t0 + 512], x7s[:, :])
            ln7.acc(x7s[:, :], t0)

    ln7.apply(x7_d, xn7, x6t)
    for c in range(NCH):
        p.dma("qs", xn7_d[c * 128:(c + 1) * 128, :], xn7[:, c * T:(c + 1) * T])

    for (w_nm, b_nm, o_d, scl) in (("qpT", "qbr", qt_d, 1.0 / 32.0),
                                   ("kpT", "kbr", kt_d, 1.0),
                                   ("vpT", "vbr", vt_d, 1.0)):
        p.dma("qg", brow[:, :], ins[b_nm])
        for ds in range(2):
            p.dma("qg", wbuf[:, 0:NCH * 512].rearrange("p (c n) -> p c n", n=512),
                  ins[w_nm][:, ds * 512:(ds + 1) * 512].rearrange("(c p) n -> p c n", p=128))
            for ti in range(16):
                t0 = ti * 128
                for kc in range(NCH):
                    p.mm(ps_a[:, :], xn7[:, kc * T + t0: kc * T + t0 + 128],
                         wbuf[:, kc * 512:(kc + 1) * 512],
                         start=(kc == 0), stop=False)
                p.mm(ps_a[:, :], onesb[0:1, :], brow[0:1, ds * 512:(ds + 1) * 512],
                     start=False, stop=True)
                p.act(ttile[:, ds * 512:ds * 512 + 512], ps_a[:, :], AF.Copy, scale=scl)
                p.dma("qs", o_d[t0:t0 + 128, ds * 512:(ds + 1) * 512],
                      ttile[:, ds * 512:ds * 512 + 512])
    ph.end()
    return ph



def build_l3flash(P=None, ext=None):
    ph = Phase(P, "lf", ext)
    nc = ph.nc
    LQ = L
    ins = {}
    for nm, shp, dt in (("Qt", [LQ, 256], BF16), ("Kt", [LQ, 256], BF16),
                        ("Vt", [LQ, 256], BF16)):
        ins[nm] = ph.dio(nm, shp, dt, "ExternalInput")
    ins["ones"] = ph.din("ones", [128, 128], F32)
    of_d = ph.dio("Of", [256, LQ], BF16, "ExternalOutput")

    p = ph.begin()
    Qs = p.sb("Qs", [128, 2 * LQ], BF16)
    Ks = p.sb("Ks", [128, 2 * LQ], BF16)
    Vs = p.sb("Vs", [128, 2 * LQ], BF16)
    Os = p.sb("Os", [128, 2 * LQ], BF16)
    Pb = p.sb("Pb", [128, 512], BF16)
    sqf = p.sb("sqf", [128, 512], BF16)
    lrow = p.sb("lrow", [1, 512], F32)
    lrb = p.sb("lrb", [1, 512], BF16)
    negc = p.sb("negc", [128, 1], F32)
    mrow = p.sb("mrow", [1, 4], F32)
    mrb = p.sb("mrb", [1, 1], BF16)
    ofp = p.sb("ofp", [128, 512], F32)
    onesb = p.sb("onesb", [128, 128], BF16)

    ps_s = p.ps("ps_s", [128, 512])
    ps_o = p.ps("ps_o", [128, 512])
    ps_l = p.ps("ps_l", [1, 512])
    ps_bc = p.ps("ps_bc", [128, 512])

    for u in range(2):
        for t0 in range(0, LQ, 2048):
            p.dma("qs", Qs[:, u * LQ + t0:u * LQ + t0 + 2048],
                  ins["Qt"][t0:t0 + 2048, u * 128:(u + 1) * 128], transpose=True)
            p.dma("qs", Ks[:, u * LQ + t0:u * LQ + t0 + 2048],
                  ins["Kt"][t0:t0 + 2048, u * 128:(u + 1) * 128], transpose=True)
        p.dma("qs", Vs[:, u * LQ:(u + 1) * LQ].rearrange("p (n f) -> p n f", f=128),
              ins["Vt"][:, u * 128:(u + 1) * 128].rearrange("(n p) f -> p n f", p=128))
    p.dma("qg", onesb[:, :], ins["ones"])

    for u in range(2):
        for (buf, col) in ((Qs, 0), (Ks, 1)):
            p.memset(mrow[0:1, col:col + 1], 0.0)
            for t0 in range(0, LQ, 512):
                p.tt(sqf[:, :], buf[:, u * LQ + t0: u * LQ + t0 + 512],
                     buf[:, u * LQ + t0: u * LQ + t0 + 512], ALU.mult)
                p.mm(ps_l[0:1, :], onesb[:, 0:1], sqf[:, :])
                p.copy(lrow[0:1, :], ps_l[0:1, :])
                p.red(mrow[0:1, 3:4], lrow[0:1, :], ALU.max)
                p.tt(mrow[0:1, col:col + 1], mrow[0:1, col:col + 1],
                     mrow[0:1, 3:4], ALU.max)
            p.act(mrow[0:1, col:col + 1], mrow[0:1, col:col + 1], AF.Sqrt)
        p.tt(mrow[0:1, 2:3], mrow[0:1, 0:1], mrow[0:1, 1:2], ALU.mult)
        p.ts(mrow[0:1, 2:3], mrow[0:1, 2:3], -1.0, None, ALU.mult)
        p.copy(mrb[:, :], mrow[0:1, 2:3])
        p.mm(ps_bc[:, 0:1], onesb[0:1, :], mrb[:, :])
        p.copy(negc[:, :], ps_bc[:, 0:1])

        for q0 in range(0, LQ, 512):
            for kt in range(LQ // 128):
                p.mm(ps_s[:, :], Ks[:, u * LQ + kt * 128: u * LQ + (kt + 1) * 128],
                     Qs[:, u * LQ + q0: u * LQ + q0 + 512])
                p.act(Pb[:, :], ps_s[:, :], AF.Exp, bias=negc[:, 0:1])
                p.mm(ps_l[0:1, :], onesb[:, 0:1], Pb[:, :], start=(kt == 0),
                     stop=(kt == LQ // 128 - 1))
                p.mm(ps_o[:, :], Vs[:, u * LQ + kt * 128: u * LQ + (kt + 1) * 128],
                     Pb[:, :], start=(kt == 0), stop=(kt == LQ // 128 - 1))
            p.copy(lrow[0:1, :], ps_l[0:1, :])
            p.recip(lrow[0:1, :], lrow[0:1, :])
            p.copy(lrb[:, :], lrow[0:1, :])
            p.mm(ps_bc[:, :], onesb[0:1, :], lrb[0:1, :])
            p.copy(ofp[:, :], ps_o[:, :])
            p.tt(ofp[:, :], ofp[:, :], ps_bc[:, :], ALU.mult)
            p.copy(Os[:, u * LQ + q0: u * LQ + q0 + 512], ofp[:, :])
    for u in range(2):
        p.dma("qs", of_d[u * 128:(u + 1) * 128, :], Os[:, u * LQ:(u + 1) * LQ])
    ph.end()
    return ph



def build_l3post(P=None, ext=None):
    ph = Phase(P, "lp", ext)
    nc = ph.nc
    T = TPC
    DIO = {"attn", "x7", "xn7"}
    ins = {}
    for nm, shp, dt in (("attn", [D, T], BF16), ("x7", [D, T], F32),
                        ("xn7", [D, T], BF16), ("opT", [D, D], F32),
                        ("obc", [D, 1], F32)):
        ins[nm] = (ph.dio(nm, shp, dt, "ExternalInput") if nm in DIO
                   else ph.din(nm, shp, dt))
    out_d = ph.dout("out", [D, T], F32)

    p = ph.begin()
    at = p.sb("at_s", [128, NCH * T], BF16)
    xn7 = p.sb("xn7_s", [128, NCH * T], BF16)
    wbuf = p.sb("wbuf", [128, NCH * 128], BF16)
    obc = p.sb("obc_s", [128, 8], F32)
    x7t = p.sb("x7t", [128, 512], F32)
    os_ = p.sb("os_", [128, 512], F32)
    ps_a = p.ps("ps_a", [128, 512])

    for c in range(NCH):
        p.dma("qs", at[:, c * T:(c + 1) * T], ins["attn"][c * 128:(c + 1) * 128, :])
        p.dma("qs", xn7[:, c * T:(c + 1) * T], ins["xn7"][c * 128:(c + 1) * 128, :])
    p.dma("qg", obc[:, :], ins["obc"].rearrange("(c p) one -> p (c one)", p=128),
          allow_slow_non_contiguous=True)

    for oc in range(NCH):
        p.dma("qg", wbuf[:, :].rearrange("p (c n) -> p c n", n=128),
              ins["opT"][:, oc * 128:(oc + 1) * 128].rearrange("(c p) n -> p c n", p=128))
        for t0 in range(0, T, 512):
            for kc in range(NCH):
                p.mm(ps_a[:, :], wbuf[:, kc * 128:(kc + 1) * 128],
                     at[:, kc * T + t0: kc * T + t0 + 512],
                     start=(kc == 0), stop=(kc == NCH - 1))
            sl_ = slice(oc * T + t0, oc * T + t0 + 512)
            p.ts(os_[:, :], ps_a[:, :], obc[:, oc:oc + 1], None, ALU.add)
            p.tt(os_[:, :], os_[:, :], xn7[:, sl_], ALU.add)
            p.dma("qs", x7t[:, :], ins["x7"][oc * 128:(oc + 1) * 128, t0:t0 + 512])
            p.tt(os_[:, :], os_[:, :], x7t[:, :], ALU.add)
            p.dma("qs", out_d[oc * 128:(oc + 1) * 128, t0:t0 + 512], os_[:, :])
    ph.end()
    return ph



MEGA_ORDER = ("p0", "d0", "f0", "p1", "d1", "m1", "fq", "ff", "fi", "fc",
              "sc", "pa", "f2", "la", "lf", "lp")

I32 = mybir.dt.int32


def build_mega(upto=None, dumps=()):
    nc = bass.Bass("TRN2", target_bir_lowering=False, debug=False)
    p = Prog(nc)
    p.bar_sb = p.sb("bar_sb", [1, 1], F32)
    p.bar_dr = nc.dram_tensor("bar_dr", [1, 1], F32, kind="Internal").ap()

    ones_d = nc.dram_tensor("ones", [128, 128], F32, kind="ExternalInput").ap()
    id_d = nc.dram_tensor("ident", [128, 128], F32, kind="ExternalInput").ap()
    sl_d = nc.dram_tensor("sl", [128, 128], F32, kind="ExternalInput").ap()
    su_d = nc.dram_tensor("su", [128, 128], F32, kind="ExternalInput").ap()
    sui_d = nc.dram_tensor("sui", [128, 128], F32, kind="ExternalInput").ap()
    idx_d = nc.dram_tensor("idx", [1, 9], I32, kind="ExternalInput").ap()
    SH = {"ones": ones_d, "ident": id_d, "sl": sl_d, "su": su_d, "sui": sui_d}

    idx_sb = p.sb("idx_sb", [1, 9], I32)
    p.dma("qg", idx_sb[:, :], idx_d)
    for j in range(9):
        p.loadreg(j, idx_sb[0:1, j:j + 1])

    def dram(nm, shape, dt):
        return nc.dram_tensor(nm, shape, dt, kind="Internal").ap()

    tok_in = dram("tok_in", [1, 1], F32)
    tok_sb = p.sb("tok_sb", [1, 1], F32)
    p.memset(tok_sb[:, :], 1.0, q="vector")
    p.dma("qs", tok_in, tok_sb[:, :])

    def ag(nm, src, dt, rows, cols):
        g = dram(nm, [rows * 8, cols], dt)
        p.coll("AllGather", ALU.bypass, [src], [g],
               reads=_tags(src), writes=_tags(g))
        tko = dram(f"tk_{nm}", [1, 1], F32)
        p.coll("AllReduce", ALU.add, [tok_in], [tko],
               reads=_tags(tok_in), writes=_tags(g) + _tags(tko))
        return g

    def dynop(dst_fn, src_fn, g, dst):
        p.op("qs", lambda e: e.dma_start(out=dst_fn(e), in_=src_fn(e)),
             reads=_tags(g), writes=_tags(dst))

    def gather4(dst, g, prows, mid_reg, nmid):
        dynop(lambda e: dst.rearrange("p (c n) -> c p n", c=4),
              lambda e: g.rearrange("(c p) n -> c p n", p=prows)[
                  bass.ds(p.regs[4], 4), bass.ds(p.regs[mid_reg], nmid), :],
              g, dst)

    def gather4f(dst, g, prows):
        dynop(lambda e: dst.rearrange("p (c n) -> c p n", c=4),
              lambda e: g.rearrange("(c p) n -> c p n", p=prows)[
                  bass.ds(p.regs[4], 4), :, :],
              g, dst)

    def dyn2(dst, g, row_reg, nrows, col_reg, ncols):
        p.op("qs", lambda e: e.dma_start(
                out=dst[:, :],
                in_=g[bass.ds(p.regs[row_reg], nrows),
                      bass.ds(p.regs[col_reg], ncols)]),
             reads=_tags(g), writes=_tags(dst))

    def d2d(dst, dst_sl, src):
        p.op("qs", lambda e: e.dma_start(
                out=dst[dst_sl] if dst_sl is not None else dst[:, :], in_=src),
             reads=_tags(src), writes=_tags(dst))

    phases = {}
    p.marks = []

    def mark(lbl):
        p.marks.append((p.gidx, lbl))

    def done(name):
        mark("phase:" + name)
        if dumps:
            for pn, dn in dumps:
                if pn == name:
                    src = phases[name].drams[dn]
                    o = nc.dram_tensor(f"dmp_{pn}_{dn}", list(src.tensor.shape),
                                       src.tensor.dtype, kind="ExternalOutput").ap()
                    d2d(o, None, src)
        if upto == name:
            p.emit()
            return True
        return False

    p0 = phases["p0"] = build_prep0(P=p, ext=SH)
    if done("p0"):
        return nc, phases
    g_qs = ag("g_qs", p0.drams["qs"], BF16, TPC, 512)
    g_kn = ag("g_kn", p0.drams["kn"], BF16, TPC, 512)
    g_kb = ag("g_kb", p0.drams["kb"], BF16, TPC, 512)
    g_vb = ag("g_vb", p0.drams["vb"], BF16, TPC, 1024)
    g_sg = ag("g_sg", p0.drams["sg"], BF16, TPC, 1024)
    my_q = dram("my_q", [L, 128], BF16)
    my_k = dram("my_k", [L, 128], BF16)
    my_kb = dram("my_kb", [L, 128], BF16)
    my_vb = dram("my_vb", [L, 256], BF16)
    my_sg = dram("my_sg", [L, 256], BF16)
    dyn2(my_q, g_qs, 0, L, 1, 128)
    dyn2(my_k, g_kn, 0, L, 1, 128)
    dyn2(my_kb, g_kb, 0, L, 1, 128)
    dyn2(my_vb, g_vb, 0, L, 2, 256)
    dyn2(my_sg, g_sg, 0, L, 2, 256)
    mark("EX1")

    d0 = phases["d0"] = build_delta(
        128, 256, post="l0", layout="tm", P=p, name="d0",
        ext={**SH, "q": my_q, "k": my_k, "kb": my_kb, "vb": my_vb, "sg": my_sg})
    if done("d0"):
        return nc, phases
    g_og0 = ag("g_og0", d0.drams["og"], BF16, L, 256)
    my_og = dram("my_og", [TPC, 1024], BF16)
    gather4(my_og, g_og0, L, 5, TPC)
    mark("EX2")

    f0 = phases["f0"] = build_wo_ffn0(
        P=p, ext={**SH, "x0": p0.drams["x"], "ogt": my_og})
    if done("f0"):
        return nc, phases
    g_xn2 = ag("g_xn2", f0.drams["xn2o"], BF16, 1024, TPC)
    my_xnh = dram("my_xnh", [D, 2560], BF16)
    dynop(lambda e: my_xnh[:, 0:512],
          lambda e: g_xn2[bass.ds(p.regs[3], 1024), TPC - 512:TPC],
          g_xn2, my_xnh)
    d2d(my_xnh, (slice(None), slice(512, 2560)), f0.drams["xn2o"])
    mark("EX3")

    p1 = phases["p1"] = build_prep1(P=p, ext={**SH, "xnh": my_xnh})
    if done("p1"):
        return nc, phases
    my1 = {}
    for nm in ("q1", "k1", "kb1", "vb1"):
        g = ag(f"g_{nm}", p1.drams[nm], BF16, 1024, TPC)
        m = dram(f"my_{nm}", [256, L], BF16)
        gather4(m, g, 1024, 2, 256)
        my1[nm] = m

    d1 = phases["d1"] = build_delta(
        256, 256, post="raw", layout="fm", P=p, name="d1",
        ext={**SH, "q": my1["q1"], "k": my1["k1"], "kb": my1["kb1"],
             "vb": my1["vb1"]})
    if done("d1"):
        return nc, phases
    g_og1 = ag("g_og1", d1.drams["og"], BF16, L, 256)
    my_delt = dram("my_delt", [TPC, 1024], BF16)
    gather4(my_delt, g_og1, L, 5, TPC)
    mark("EX5")

    m1 = phases["m1"] = build_mix1(
        P=p, ext={**SH, "x2": f0.drams["x2"], "xn2": f0.drams["xn2o"],
                  "lsf": p1.drams["ls"], "llf": p1.drams["ll"],
                  "vdf": p1.drams["vd"], "delt": my_delt})
    if done("m1"):
        return nc, phases

    fq = phases["fq"] = build_ffn1qk(P=p, ext={**SH, "xn3": m1.drams["xn3"]})
    if done("fq"):
        return nc, phases

    ff = phases["ff"] = build_ffn1branch(
        P=p, name="ff", ext={**SH, "qside": fq.drams["Q"], "kside": fq.drams["K"]})
    if done("ff"):
        return nc, phases
    fi = phases["fi"] = build_ffn1branch(
        P=p, name="fi", ext={**SH, "qside": fq.drams["iQ"], "kside": fq.drams["iK"]})
    if done("fi"):
        return nc, phases
    my_tot = {}
    for nm, src in (("totf", ff.drams["tot"]), ("toti", fi.drams["tot"])):
        g = ag(f"g_{nm}", src, F32, 512, 1)
        m = dram(f"my_{nm}", [8, 512], F32)
        p.op("qs", lambda e, m=m, g=g: e.dma_start(
                out=m[:, :], in_=g.rearrange("(r c) one -> r (c one)", r=8)),
             reads=_tags(g), writes=_tags(m))
        my_tot[nm] = m

    fc = phases["fc"] = build_ffn1b(
        P=p, ext={**SH, "xV_f": ff.drams["xV"], "cum_f": ff.drams["cum"],
                  "xV_i": fi.drams["xV"], "cum_i": fi.drams["cum"],
                  "wts_f": ff.drams["wts"], "wts_i": fi.drams["wts"],
                  "totf": my_tot["totf"], "toti": my_tot["toti"],
                  "x3": m1.drams["x3"]})
    if done("fc"):
        return nc, phases
    g_bx = ag("g_bx", fc.drams["bx"], F32, 64, TPC)
    g_ih = ag("g_ih", fc.drams["ih"], F32, 384, TPC)
    my_bx = dram("my_bx", [64, L], F32)
    my_ih = dram("my_ih", [384, L], F32)
    gather4f(my_bx, g_bx, 64)
    gather4f(my_ih, g_ih, 384)

    sc = phases["sc"] = build_scan(L, P=p, ext={**SH, "bx": my_bx, "ih": my_ih})
    if done("sc"):
        return nc, phases
    my_hs = dram("my_hs", [64, TPC], BF16)
    my_hr = dram("my_hr", [128, TPC], BF16)
    hsh_d, hrh_d = sc.drams["hsh"], sc.drams["hrh"]
    dynop(lambda e: my_hs[:, :], lambda e: hsh_d[0:64, bass.ds(p.regs[5], TPC)],
          hsh_d, my_hs)
    dynop(lambda e: my_hr[:, :], lambda e: hrh_d[0:128, bass.ds(p.regs[5], TPC)],
          hrh_d, my_hr)
    mark("EX8")

    pa = phases["pa"] = build_post2a(
        P=p, ext={**SH, "x4": fc.drams["x4"], "xn4": fc.drams["xn4"],
                  "hs": my_hs, "hr": my_hr})
    if done("pa"):
        return nc, phases

    f2 = phases["f2"] = build_ffn2(
        P=p, ext={**SH, "x5": pa.drams["x5"], "xn5": pa.drams["xn5"]})
    if done("f2"):
        return nc, phases
    g_xn6 = ag("g_xn6", f2.drams["xn6"], BF16, 1024, TPC)
    my_xnA = dram("my_xnA", [D, TPC], BF16)
    my_xnB = dram("my_xnB", [D, TPC], BF16)
    dynop(lambda e: my_xnA[:, :], lambda e: g_xn6[bass.ds(p.regs[6], 1024), :],
          g_xn6, my_xnA)
    dynop(lambda e: my_xnB[:, :], lambda e: g_xn6[bass.ds(p.regs[7], 1024), :],
          g_xn6, my_xnB)
    mark("EX9")

    la = phases["la"] = build_l3a(
        P=p, ext={**SH, "x6": f2.drams["x6"], "xnA": my_xnA, "xnB": my_xnB})
    if done("la"):
        return nc, phases
    my_qkv = {}
    for nm in ("qt", "kt", "vt"):
        g = ag(f"g_{nm}", la.drams[nm], BF16, TPC, 1024)
        m = dram(f"my_{nm}", [L, 256], BF16)
        dyn2(m, g, 0, L, 2, 256)
        my_qkv[nm] = m

    lf = phases["lf"] = build_l3flash(
        P=p, ext={**SH, "Qt": my_qkv["qt"], "Kt": my_qkv["kt"],
                  "Vt": my_qkv["vt"]})
    if done("lf"):
        return nc, phases
    g_of = ag("g_of", lf.drams["Of"], BF16, 256, L)
    my_attn = dram("my_attn", [D, TPC], BF16)
    dyn2(my_attn, g_of, 8, 1024, 5, TPC)
    mark("EX11")

    lp = phases["lp"] = build_l3post(
        P=p, ext={**SH, "attn": my_attn, "x7": la.drams["x7"],
                  "xn7": la.drams["xn7"]})
    if dumps:
        for pn, dn in dumps:
            if pn == "lp":
                src = lp.drams[dn]
                o = nc.dram_tensor(f"dmp_lp_{dn}", list(src.tensor.shape),
                                   src.tensor.dtype, kind="ExternalOutput").ap()
                d2d(o, None, src)
    p.emit()
    return nc, phases


def mega_maps(inp):
    import ml_dtypes
    BFn = ml_dtypes.bfloat16
    ones = np.ones((128, 128), np.float32)
    ident = np.eye(128, dtype=np.float32)
    i_ = np.arange(128)
    sl_m = (i_[:, None] > i_[None, :]).astype(np.float32)
    su_m = (i_[:, None] < i_[None, :]).astype(np.float32)
    sui_m = (i_[:, None] <= i_[None, :]).astype(np.float32)
    x0 = inp["L0_attn_x"]
    E8 = np.zeros((8, 512), np.float32)
    for k in range(8):
        E8[k, k * 64:(k + 1) * 64] = 1.0
    V_f = np.ascontiguousarray(inp["L1_ffn_V_fwd"].transpose(1, 0, 2).reshape(1024, 512))
    W_f = np.ascontiguousarray(inp["L1_ffn_W_fwd"].transpose(1, 0, 2).reshape(1024, 512))
    V_i = np.ascontiguousarray(inp["L1_ffn_W_inv_exp"].transpose(1, 0, 2).reshape(1024, 512))
    W_i = np.ascontiguousarray(inp["L1_ffn_V_inv"].transpose(1, 0, 2).reshape(1024, 512))
    w1 = np.concatenate([inp["L2_ffn_w1"], inp["L2_ffn_new_w1"]], axis=0)
    w2 = np.concatenate([inp["L2_ffn_w2"], inp["L2_ffn_new_w2"]], axis=0)
    w3 = np.concatenate([inp["L2_ffn_w3"], inp["L2_ffn_new_w3"]], axis=0)
    ew1T = np.ascontiguousarray(w1.transpose(0, 2, 1).reshape(9 * 1024, 2816))
    ew3T = np.ascontiguousarray(w3.transpose(0, 2, 1).reshape(9 * 1024, 2816))
    ew2T = np.ascontiguousarray(w2.transpose(0, 2, 1).reshape(9 * 2816, 1024))
    g9T = np.ascontiguousarray(np.concatenate(
        [inp["L2_ffn_gate_weight"], inp["L2_ffn_new_gate_weight"]], axis=0).T)
    g9b = np.concatenate([np.zeros(8, np.float32), inp["L2_ffn_gate_bias"]])[None, :]
    rtr = dict(
        rw1T=np.ascontiguousarray(inp["L1_ffn_router_w1"].T),
        rb1=inp["L1_ffn_router_b1"][:, None],
        rw2T=np.ascontiguousarray(inp["L1_ffn_router_w2"].T),
        rb2=inp["L1_ffn_router_b2"][None, :],
        eb=inp["L1_ffn_expert_bias"][None, :])
    maps = []
    for core in range(NCORES):
        b, h = core // 4, core % 4
        sl_t = slice(h * TPC, (h + 1) * TPC)
        idx = np.array([[
            b * 8192, h * 128, h * 256, max(core - 1, 0) * 1024, b * 4,
            h * 2048, h * 1024, (4 + h) * 1024, b * 1024]], np.int32)
        m = {"ones": ones, "ident": ident, "sl": sl_m, "su": su_m, "sui": sui_m,
             "idx": idx}
        m.update({
            "p0_x": np.ascontiguousarray(x0[b, sl_t, :].T),
            "p0_wq": inp["L0_attn_Wq"], "p0_wk": inp["L0_attn_Wk"],
            "p0_wv": inp["L0_attn_Wv"], "p0_wb": inp["L0_attn_Wb"],
            "p0_wg": inp["L0_attn_Wg"],
            "d0_nw": inp["L0_attn_norm_w"][None, :],
            "f0_wo": inp["L0_attn_Wo"],
            "f0_w1T": np.ascontiguousarray(inp["L0_ffn_w1_w"].T),
            "f0_w2T": np.ascontiguousarray(inp["L0_ffn_w2_w"].T),
            "f0_h1T": np.ascontiguousarray(inp["L0_ffn_h1_w"].T),
            "f0_h2T": np.ascontiguousarray(inp["L0_ffn_h2_w"].T),
            "f0_w3T": np.ascontiguousarray(inp["L0_ffn_w3_w"].T),
            "f0_s1": inp["L0_ffn_w1_s"][:, None], "f0_s2": inp["L0_ffn_w2_s"][:, None],
            "f0_sh1": inp["L0_ffn_h1_s"][:, None], "f0_sh2": inp["L0_ffn_h2_s"][:, None],
            "f0_s3": inp["L0_ffn_w3_s"][:, None],
            "p1_hmask": np.full((128, 1), 0.0 if h == 0 else 1.0, np.float32),
            "p1_wq": inp["L1_attn_Wq"], "p1_wk": inp["L1_attn_Wk"],
            "p1_wv": inp["L1_attn_Wv"], "p1_wb": inp["L1_attn_Wb"],
            "p1_cq": inp["L1_attn_wq_conv"], "p1_ck": inp["L1_attn_wk_conv"],
            "p1_cv": inp["L1_attn_wv_conv"],
            "p1_f5": inp["L1_attn_fir_short"].reshape(1024, 5),
            "p1_f64": inp["L1_attn_fir_long"].reshape(1024, 64),
            "m1_wg1x": np.ascontiguousarray(inp["L1_attn_Wg1"][0:1024]),
            "m1_wg1s": np.ascontiguousarray(inp["L1_attn_Wg1"][1024:1040]),
            "m1_bg1": inp["L1_attn_bg1"][:, None],
            "m1_wg2": inp["L1_attn_Wg2"], "m1_bg2": inp["L1_attn_bg2"][None, :],
            "m1_ltemp": inp["L1_attn_logit_temp"][:, None],
            "m1_crl": inp["L1_attn_conv_res_logit"][None, :],
            "m1_wres": inp["L1_attn_Wres"], "m1_bres": inp["L1_attn_bres"][None, :],
            "m1_wog1x": np.ascontiguousarray(inp["L1_attn_Wog1"][0:1024]),
            "m1_wog1s": np.ascontiguousarray(inp["L1_attn_Wog1"][1024:1032]),
            "m1_bog1": inp["L1_attn_bog1"][:, None],
            "m1_wog2": inp["L1_attn_Wog2"], "m1_bog2": inp["L1_attn_bog2"][None, :],
            "m1_nw4": np.tile(inp["L1_attn_norm_w"], 4)[None, :],
            "m1_wo": inp["L1_attn_Wo"],
            "fq_wqT": np.ascontiguousarray(inp["L1_ffn_W_Q"].T),
            "fq_wkT": np.ascontiguousarray(inp["L1_ffn_W_K"].T),
            "fq_winvT": np.ascontiguousarray(inp["L1_ffn_W_inv"].T),
            "ff_vmat": V_f, "ff_wmat": W_f,
            "fi_vmat": V_i, "fi_wmat": W_i,
            "fc_maskr": np.array([[1.0 if (i // 4 == b and i % 4 < h) else 0.0]
                                  for i in range(8)], np.float32),
            "fc_Uf": np.ascontiguousarray(
                inp["L1_ffn_U_fwd"].transpose(0, 2, 1).reshape(512, 1024)),
            "fc_bfm": inp["L1_ffn_b_fwd"],
            "fc_Ui": np.ascontiguousarray(
                inp["L1_ffn_U_inv"].transpose(0, 2, 1).reshape(512, 1024)),
            "fc_bim": inp["L1_ffn_b_inv"],
            "fc_woT": np.ascontiguousarray(inp["L1_ffn_W_O"].T),
            "fc_alpha": np.float32(inp["L1_ffn_alpha_bi"]).reshape(1, 1),
            "fc_E8": E8,
            "fc_BwT": np.ascontiguousarray(inp["L2_attn_Bw"].T),
            "fc_Bb": inp["L2_attn_Bb"][:, None],
            "fc_WihT": np.ascontiguousarray(inp["L2_attn_W_ih"].T),
            "fc_bih": inp["L2_attn_b_ih"][:, None],
            "fc_bhh_rz": np.ascontiguousarray(inp["L2_attn_b_hh"][:256, None]),
            "fc_toff": np.float32(h * 2048).reshape(1, 1),
            "sc_AT": np.ascontiguousarray(inp["L2_attn_A"].T),
            "sc_WhT": np.ascontiguousarray(inp["L2_attn_W_hh"].T),
            "sc_bhhn": np.ascontiguousarray(inp["L2_attn_b_hh"][256:384][None, :]),
            "sc_hs0": np.zeros((64, 1), np.float32),
            "sc_hr0": np.zeros((128, 1), np.float32),
            "pa_CwT": np.ascontiguousarray(inp["L2_attn_Cw"].T),
            "pa_DwT": np.ascontiguousarray(inp["L2_attn_Dw"].T),
            "pa_cdb": inp["L2_attn_Cb"][:, None],
            "pa_cb": inp["L2_attn_Cb"][:, None], "pa_db": inp["L2_attn_Db"][:, None],
            "pa_projT": np.ascontiguousarray(inp["L2_attn_proj_w"].T),
            "pa_projb": inp["L2_attn_proj_b"][:, None],
            "pa_gateT": np.ascontiguousarray(inp["L2_attn_gate_w"].T),
            "pa_gateb": inp["L2_attn_gate_b"][:, None],
            "f2_g9T": g9T, "f2_g9b": g9b,
            "f2_ew1T": ew1T, "f2_ew3T": ew3T, "f2_ew2T": ew2T,
        })
        for k, v in rtr.items():
            m[f"ff_{k}"] = v
            m[f"fi_{k}"] = v
        mm = 1024 >> h
        qpT = np.zeros((1024, 1024), np.float32)
        qpT[:mm, :] = inp["L3_ffn_q_w"][:, :mm].T
        kpT = np.zeros((1024, 1024), np.float32)
        kpT[:mm, :] = inp["L3_ffn_k_w"][:, :mm].T
        vpT = np.zeros((1024, 1024), np.float32)
        vpT[:mm, :] = inp["L3_ffn_v_w"][:, :mm].T
        c = [1.0, 0.0, 1.0, 0.0] if b == 0 else [0.0, 1.0, 0.0, 1.0]
        opT = np.zeros((1024, 1024), np.float32)
        opT[:mm, :mm] = inp["L3_ffn_o_w"][:mm, :mm].T
        obc = np.zeros((1024, 1), np.float32)
        obc[:mm, 0] = inp["L3_ffn_o_b"][:mm]
        m.update({
            "la_c1": np.float32(c[0]).reshape(1, 1), "la_c2": np.float32(c[1]).reshape(1, 1),
            "la_c3": np.float32(c[2]).reshape(1, 1), "la_c4": np.float32(c[3]).reshape(1, 1),
            "la_qpT": qpT, "la_kpT": kpT, "la_vpT": vpT,
            "la_qbr": inp["L3_ffn_q_b"][None, :], "la_kbr": inp["L3_ffn_k_b"][None, :],
            "la_vbr": inp["L3_ffn_v_b"][None, :],
            "lp_opT": opT, "lp_obc": obc,
        })
        maps.append(m)
    return maps


def mega_trim(maps, phases):
    keep = {"ones", "ident", "sl", "su", "sui", "idx"}
    for name, ph in phases.items():
        for nm, ap in ph.drams.items():
            keep.add(ap.tensor.name)
    return [{k: v for k, v in m.items() if k in keep} for m in maps]




_MEGA_CACHE = {}


def kernel(**inputs):
    if "prog" not in _MEGA_CACHE:
        _MEGA_CACHE["prog"] = build_mega()
    nc, phases = _MEGA_CACHE["prog"]
    maps = mega_trim(mega_maps(inputs), phases)
    res = run_bass_kernel_spmd(nc, maps, core_ids=list(range(NCORES)))
    out = np.zeros((B, L, D), np.float32)
    for core in range(NCORES):
        b, h = core // 4, core % 4
        out[b, h * TPC:(h + 1) * TPC, :] = np.asarray(
            res.results[core]["lp_out"], np.float32).T
    return out
